# version 06
# baseline (speedup 1.0000x reference)
import os
from collections import defaultdict
from contextlib import ExitStack

import numpy as np
import ml_dtypes

import concourse.bass as bass
import concourse.mybir as mybir
from concourse.bass_utils import run_bass_kernel_spmd

F32 = mybir.dt.float32
BF16 = mybir.dt.bfloat16
AF = mybir.ActivationFunctionType
ALU = mybir.AluOpType
AX = mybir.AxisListType

D = 1024
S = 4096
Q = 2048
NH = 16
HD = 64
E = 8
F = 2816
FGROUPS = [5, 5, 4, 4, 4]
NEG_BIG = -1.0e30
EPS = 1e-5

ENGS = ("t", "v", "s", "y", "g")


class Prog:

    def __init__(self):
        self.q = {e: [] for e in ENGS}
        self.cnt = defaultdict(int)
        self.sync_mode = False

    def emit(self, eng, fn, waits=(), inc=0):
        if self.sync_mode and fn is not None:
            waits = tuple(waits) + tuple((o, self.cnt[o]) for o in ENGS if o != eng)
        ws = []
        for w in waits:
            if w is None:
                continue
            sn, c = w
            assert c <= self.cnt[sn], f"impossible wait {sn}>={c} (at {self.cnt[sn]})"
            if sn == eng and fn is not None:
                continue
            ws.append((sn, c))
        if inc:
            self.cnt[eng] += inc
        self.q[eng].append((fn, tuple(ws), inc))
        return (eng, self.cnt[eng])

    def t(self, fn, waits=(), inc=1):
        return self.emit("t", fn, waits, inc)

    def v(self, fn, waits=(), inc=1):
        return self.emit("v", fn, waits, inc)

    def s(self, fn, waits=(), inc=1):
        return self.emit("s", fn, waits, inc)

    def y(self, fn, waits=(), inc=16):
        return self.emit("y", fn, waits, inc)

    def g(self, fn, waits=(), inc=16):
        return self.emit("g", fn, waits, inc)

    def barrier(self):
        marks = {}
        for e in ENGS:
            if e in ("y", "g"):
                marks[e] = (e, self.cnt[e])
            else:
                marks[e] = self.emit(e, lambda h: h.nop(), (), 1)
        for e in ENGS:
            self.emit(e, None, tuple(marks[o] for o in ENGS if o != e), 0)

    def replay(self, nc, sems):
        handles = {"t": nc.tensor, "v": nc.vector, "s": nc.scalar,
                   "y": nc.sync, "g": nc.gpsimd}
        waited = {e: set() for e in ENGS}
        for e in ENGS:
            for fn, waits, inc in self.q[e]:
                for sn, c in waits:
                    waited[sn].add(c)

        def run_engine(eng):
            h = handles[eng]
            is_dma_q = eng in ("y", "g")
            cum = 0
            pending = 0
            for fn, waits, inc in self.q[eng]:
                for sn, c in waits:
                    h.wait_ge(sems[sn], c)
                if fn is None:
                    continue
                ins = fn(h)
                cum += inc
                pending += inc
                if pending and (is_dma_q or cum in waited[eng]):
                    ins.then_inc(sems[eng], pending)
                    pending = 0
        return run_engine


def _finalize(nc, p, sems, ctx):
    with nc.Block() as block:
        run = p.replay(nc, sems)
        block.sync(lambda h: run("y"))
        block.tensor(lambda h: run("t"))
        block.vector(lambda h: run("v"))
        block.scalar(lambda h: run("s"))
        block.gpsimd(lambda h: run("g"))
    ctx.close()
    return nc


def rope_tables():
    base = np.float32(10000.0) * np.float32(S / 2048.0) ** np.float32(HD / (HD - 2))
    inv_freq = (1.0 / base ** (np.arange(0, HD, 2, dtype=np.float32) / np.float32(HD)
                               )).astype(np.float32)
    t = np.arange(S, dtype=np.float32)
    freqs = np.einsum("i,j->ij", t, inv_freq).astype(np.float32)
    emb = np.concatenate([freqs, freqs], axis=-1)
    return np.cos(emb).astype(np.float32), np.sin(emb).astype(np.float32)


def build_program(phases=5):
    nc = bass.Bass("TRN2", target_bir_lowering=False, detect_race_conditions=False)

    xi = nc.dram_tensor("x_full", [S, D], F32, kind="ExternalInput").ap()
    Wq = nc.dram_tensor("Wq", [D, D], F32, kind="ExternalInput").ap()
    Wk = nc.dram_tensor("Wk", [D, D], F32, kind="ExternalInput").ap()
    Wv = nc.dram_tensor("Wv", [D, D], F32, kind="ExternalInput").ap()
    Wo = nc.dram_tensor("Wo", [D, D], F32, kind="ExternalInput").ap()
    gwTin = nc.dram_tensor("gwTin", [128, 8 * E], F32, kind="ExternalInput").ap()
    wgate = nc.dram_tensor("w_gate", [E, F, D], F32, kind="ExternalInput").ap()
    wup = nc.dram_tensor("w_up", [E, F, D], F32, kind="ExternalInput").ap()
    wdown = nc.dram_tensor("w_down", [E, D, F], F32, kind="ExternalInput").ap()
    cosq8 = nc.dram_tensor("cosq8", [Q, HD], F32, kind="ExternalInput").ap()
    sinq8 = nc.dram_tensor("sinq8", [Q, HD], F32, kind="ExternalInput").ap()
    cosk = nc.dram_tensor("cosk", [S, HD], F32, kind="ExternalInput").ap()
    sink = nc.dram_tensor("sink", [S, HD], F32, kind="ExternalInput").ap()
    identb = nc.dram_tensor("identb", [128, 128], BF16, kind="ExternalInput").ap()
    identf = nc.dram_tensor("identf", [128, 128], F32, kind="ExternalInput").ap()
    onesf = nc.dram_tensor("onesf", [1, 128], F32, kind="ExternalInput").ap()
    out = nc.dram_tensor("out", [Q, D], F32, kind="ExternalOutput").ap()

    _TAPS = bool(os.environ.get("KDBG_TAPS"))
    _skind = "ExternalOutput" if _TAPS else "Internal"
    KT_s = nc.dram_tensor("KT_s", [8, 128, S], BF16, kind=_skind).ap()
    V_s = nc.dram_tensor("V_s", [S // 128, 128, D], BF16, kind=_skind).ap()
    QT_s = nc.dram_tensor("QT_s", [128, 8 * Q], BF16, kind=_skind).ap()

    p = Prog()
    if os.environ.get("KSYNC_ALL"):
        p.sync_mode = True
    ctx = ExitStack()
    e = ctx.enter_context
    sems = {n: e(nc.semaphore(n)) for n in ENGS}

    idb = e(nc.sbuf_tensor("idb", [128, 128], BF16))
    idf = e(nc.sbuf_tensor("idf", [128, 128], F32))
    ones_sb = e(nc.sbuf_tensor("ones_sb", [1, 128], F32))
    gwT = e(nc.sbuf_tensor("gwT", [128, 8 * E], F32))
    comb = e(nc.sbuf_tensor("comb", [128, 16 * E], F32))
    m_all = e(nc.sbuf_tensor("m_all", [128, NH * 16], F32))
    z_all = e(nc.sbuf_tensor("z_all", [128, NH * 16], F32))
    junk1 = e(nc.sbuf_tensor("junk1", [128, 1], F32))

    ytot_ctx = ExitStack()
    y_tot = ytot_ctx.enter_context(nc.sbuf_tensor("y_tot", [128, 16 * D], F32))
    xnq_ctx = ExitStack()
    xnq = xnq_ctx.enter_context(nc.sbuf_tensor("xnq", [128, 8 * Q], BF16))
    xnkv_ctx = ExitStack()
    xnkv = xnkv_ctx.enter_context(nc.sbuf_tensor("xnkv", [128, 8 * Q], BF16))

    p.v(lambda h: h.memset(junk1[:], 0.0))
    def finish_early():
        with nc.sbuf_tensor("zout", [128, D], F32) as zout:
            zv = p.v(lambda h: h.memset(zout[:], 0.0))
            last = None
            for tokc in range(16):
                last = p.y(lambda h, tokc=tokc: h.dma_start(
                    out=out[tokc * 128:(tokc + 1) * 128, :], in_=zout[:]),
                    waits=(zv,))
            p.emit("y", None, (last,), 0)

    p.y(lambda h: h.dma_start(out=idb[:], in_=identb[:, :]))
    p.y(lambda h: h.dma_start(out=idf[:], in_=identf[:, :]))
    p.y(lambda h: h.dma_start(out=ones_sb[:], in_=onesf[:, :]))
    y_init = ("y", p.cnt["y"])

    v_gw = p.y(lambda h: h.dma_start(out=gwT[:], in_=gwTin[:, :]))

    p1s = ExitStack()
    e1 = p1s.enter_context
    xti = [e1(nc.sbuf_tensor(f"xt{i}", [128, D], F32)) for i in range(3)]
    xnt = [e1(nc.sbuf_tensor(f"xnt{i}", [128, D], F32)) for i in range(2)]
    sqscr = e1(nc.sbuf_tensor("sqscr", [128, D], F32))
    xnfscr = [e1(nc.sbuf_tensor(f"xnfscr{i}", [128, D], F32)) for i in range(2)]
    s1t = e1(nc.sbuf_tensor("s1t", [128, 1], F32))
    s2t = e1(nc.sbuf_tensor("s2t", [128, 1], F32))
    mt = e1(nc.sbuf_tensor("mt", [128, 1], F32))
    vt = e1(nc.sbuf_tensor("vt", [128, 1], F32))
    sdt = e1(nc.sbuf_tensor("sdt", [128, 1], F32))
    r0t = e1(nc.sbuf_tensor("r0t", [128, 1], F32))
    rat = e1(nc.sbuf_tensor("rat", [128, 1], F32))
    rbt = e1(nc.sbuf_tensor("rbt", [128, 1], F32))
    rt = e1(nc.sbuf_tensor("rt", [128, 1], F32))
    nbt = e1(nc.sbuf_tensor("nbt", [128, 1], F32))
    gst = e1(nc.sbuf_tensor("gst", [128, E], F32))
    m1t = e1(nc.sbuf_tensor("m1t", [128, 1], F32))
    m2t = e1(nc.sbuf_tensor("m2t", [128, 1], F32))
    mask1 = e1(nc.sbuf_tensor("mask1", [128, E], F32))
    mask2 = e1(nc.sbuf_tensor("mask2", [128, E], F32))
    gsmt = e1(nc.sbuf_tensor("gsmt", [128, E], F32))
    p1t = e1(nc.sbuf_tensor("p1t", [128, 1], F32))
    p2t = e1(nc.sbuf_tensor("p2t", [128, 1], F32))
    pxn = [e1(nc.psum_tensor(f"pxn{i}", [128, D], F32)) for i in range(2)]
    pgs = e1(nc.psum_tensor("pgs", [128, E], F32))
    stats_sb = e1(nc.sbuf_tensor("stats_sb", [128, 128], F32)) if _TAPS else None
    stats2_sb = e1(nc.sbuf_tensor("stats2_sb", [128, 320], F32)) if _TAPS else None
    stats3_sb = e1(nc.sbuf_tensor("stats3_sb", [128, 192], F32)) if _TAPS else None

    load_t, norm_t, s_last, xpose_t, evac_t, gate_t, comb_t = {}, {}, {}, {}, {}, {}, {}
    for tt in range(32):
        slot = tt % 3
        load_t[tt] = p.y(lambda h, tt=tt, slot=slot: h.dma_start(
            out=xti[slot][:], in_=xi[tt * 128:(tt + 1) * 128, :]),
            waits=(s_last.get(tt - 3), norm_t.get(tt - 3)))
        xt = xti[slot]
        p.v(lambda h, xt=xt: h.reduce_sum(s1t[:], xt[:], axis=AX.X),
            waits=(load_t[tt], s_last.get(tt - 1)))
        sq = p.s(lambda h, xt=xt: h.activation(sqscr[:], xt[:], AF.Square,
                                               accum_out=s2t[:]),
                 waits=(load_t[tt], comb_t.get(tt - 1) or evac_t.get(tt - 1)))
        p.v(lambda h: h.tensor_scalar(vt[:], s2t[:], 1.0 / D, EPS, ALU.mult,
                                      ALU.add), waits=(sq,))
        p.v(lambda h: h.tensor_scalar_mul(mt[:], s1t[:], 1.0 / D))
        p.v(lambda h: h.tensor_copy(junk1[:], s1t[:]))
        p.v(lambda h: h.tensor_mul(rat[:], mt[:], mt[:]))
        p.v(lambda h: h.tensor_copy(junk1[:], s1t[:]))
        vsd = p.v(lambda h: h.tensor_sub(vt[:], vt[:], rat[:]))
        ssd = p.s(lambda h: h.activation(sdt[:], vt[:], AF.Sqrt), waits=(vsd,))
        s_last[tt] = ssd
        p.v(lambda h: h.reciprocal(r0t[:], sdt[:]), waits=(ssd,))
        p.v(lambda h: h.tensor_copy(junk1[:], s1t[:]))
        p.v(lambda h: h.tensor_mul(rat[:], r0t[:], r0t[:]))
        p.v(lambda h: h.tensor_copy(junk1[:], s1t[:]))
        p.v(lambda h: h.tensor_mul(rbt[:], rat[:], vt[:]))
        p.v(lambda h: h.tensor_copy(junk1[:], s1t[:]))
        p.v(lambda h: h.tensor_scalar(rbt[:], rbt[:], -0.5, 1.5, ALU.mult, ALU.add))
        p.v(lambda h: h.tensor_copy(junk1[:], s1t[:]))
        p.v(lambda h: h.tensor_mul(rt[:], r0t[:], rbt[:]))
        p.v(lambda h: h.tensor_copy(junk1[:], s1t[:]))
        vnb = p.v(lambda h: h.scalar_tensor_tensor(nbt[:], mt[:], -1.0, rt[:],
                                                   ALU.mult, ALU.mult))
        xn = xnt[tt % 2]
        norm_t[tt] = p.v(lambda h, xt=xt, xn=xn: h.tensor_scalar(
            xn[:], xt[:], rt[:, 0:1], nbt[:, 0:1], ALU.mult, ALU.add),
            waits=(vnb, xpose_t.get(tt - 2)))
        if _TAPS:
            p.v(lambda h, tt=tt: h.tensor_copy(stats_sb[:, tt:tt + 1], rt[:]))
            p.v(lambda h, tt=tt: h.tensor_copy(stats_sb[:, 32 + tt:33 + tt], nbt[:]))
            p.v(lambda h, tt=tt: h.tensor_copy(stats_sb[:, 64 + tt:65 + tt], s1t[:]))
            p.v(lambda h, tt=tt: h.tensor_copy(stats_sb[:, 96 + tt:97 + tt], s2t[:]))
        px = pxn[tt % 2]
        tk = None
        for ic in range(8):
            tk = p.t(lambda h, xn=xn, px=px, ic=ic: h.transpose(
                px[:, ic * 128:(ic + 1) * 128], xn[:, ic * 128:(ic + 1) * 128], idf[:]),
                waits=(norm_t[tt], evac_t.get(tt - 2), y_init) if ic == 0 else ())
        xpose_t[tt] = tk
        dstq = xnq if tt < 16 else xnkv
        off = (tt if tt < 16 else tt - 16) * 128
        dview = dstq[:].rearrange("p (ic t) -> p ic t", ic=8)
        pview = px[:].rearrange("p (ic t) -> p ic t", ic=8)
        ev = p.v(lambda h, dview=dview, pview=pview, off=off: h.tensor_copy(
            dview[:, :, off:off + 128], pview), waits=(xpose_t[tt],))
        if tt < 16:
            xnf = xnfscr[tt % 2]
            ev = p.v(lambda h, xnf=xnf, px=px: h.tensor_copy(xnf[:], px[:]),
                     waits=(gate_t.get(tt - 2),))
        evac_t[tt] = ev
        if tt < 16:
            xnf = xnfscr[tt % 2]
            gtk = None
            for ic in range(8):
                gtk = p.t(lambda h, xnf=xnf, ic=ic: h.matmul(
                    pgs[:], xnf[:, ic * 128:(ic + 1) * 128],
                    gwT[:, ic * 8:(ic + 1) * 8], start=(ic == 0), stop=(ic == 7)),
                    waits=(evac_t[tt], v_gw, comb_t.get(tt - 1)) if ic == 0 else ())
            gate_t[tt] = gtk
            p.v(lambda h: h.tensor_copy(gst[:], pgs[:]), waits=(gtk,))
            p.v(lambda h: h.reduce_max(m1t[:], pgs[:], axis=AX.X))
            p.v(lambda h: h.tensor_copy(junk1[:], m1t[:]))
            p.v(lambda h: h.tensor_scalar(mask1[:], gst[:], m1t[:, 0:1], None, ALU.is_ge))
            p.v(lambda h: h.tensor_copy(junk1[:], m1t[:]))
            p.v(lambda h: h.scalar_tensor_tensor(gsmt[:], mask1[:], NEG_BIG, gst[:],
                                                 ALU.mult, ALU.add))
            p.v(lambda h: h.tensor_copy(junk1[:], m1t[:]))
            vm2 = p.v(lambda h: h.reduce_max(m2t[:], gsmt[:], axis=AX.X))
            p.v(lambda h: h.tensor_copy(junk1[:], m2t[:]))
            vmk2 = p.v(lambda h: h.tensor_scalar(mask2[:], gsmt[:], m2t[:, 0:1], None,
                                                 ALU.is_ge))
            p.s(lambda h: h.activation(p1t[:], m2t[:], AF.Sigmoid,
                                       bias=m1t[:, 0:1], scale=-1.0), waits=(vm2,))
            sp2 = p.s(lambda h: h.activation(p2t[:], m1t[:], AF.Sigmoid,
                                             bias=m2t[:, 0:1], scale=-1.0))
            s_last[tt] = sp2
            cs = comb[:, tt * E:(tt + 1) * E]
            p.v(lambda h, cs=cs: h.tensor_scalar(cs, mask1[:], p1t[:, 0:1], None,
                                                 ALU.mult), waits=(sp2, vmk2))
            p.v(lambda h: h.tensor_copy(junk1[:], m2t[:]))
            comb_t[tt] = p.v(lambda h, cs=cs: h.scalar_tensor_tensor(
                cs, mask2[:], p2t[:, 0:1], cs, ALU.mult, ALU.add))
            if _TAPS:
                c0 = tt * 20
                p.v(lambda h, c0=c0: h.tensor_copy(
                    stats2_sb[:, c0:c0 + 8], gst[:]))
                p.v(lambda h, c0=c0: h.tensor_copy(
                    stats2_sb[:, c0 + 8:c0 + 16], mask1[:]))
                p.v(lambda h, c0=c0: h.tensor_copy(
                    stats2_sb[:, c0 + 16:c0 + 17], m1t[:]))
                p.v(lambda h, c0=c0: h.tensor_copy(
                    stats2_sb[:, c0 + 17:c0 + 18], m2t[:]))
                p.v(lambda h, c0=c0: h.tensor_copy(
                    stats2_sb[:, c0 + 18:c0 + 19], p1t[:]))
                comb_t[tt] = p.v(lambda h, c0=c0: h.tensor_copy(
                    stats2_sb[:, c0 + 19:c0 + 20], p2t[:]))
                if tt in (0, 7, 15):
                    k3 = {0: 0, 7: 64, 15: 128}[tt]
                    comb_t[tt] = p.v(lambda h, k3=k3: h.tensor_copy(
                        stats3_sb[:, k3:k3 + 64], gwT[:]))

    if _TAPS:
        dbg_st = nc.dram_tensor("dbg_st", [128, 128], F32,
                                kind="ExternalOutput").ap()
        dbg_c = nc.dram_tensor("dbg_c", [128, 16 * E], F32,
                               kind="ExternalOutput").ap()
        p.barrier()
        p.y(lambda h: h.dma_start(out=dbg_st[:, :], in_=stats_sb[:]))
        p.y(lambda h: h.dma_start(out=dbg_c[:, :], in_=comb[:]))
        dbg_g2 = nc.dram_tensor("dbg_g2", [128, 320], F32,
                                kind="ExternalOutput").ap()
        p.y(lambda h: h.dma_start(out=dbg_g2[:, :], in_=stats2_sb[:]))
        dbg_gw = nc.dram_tensor("dbg_gw", [128, 8 * E], F32,
                                kind="ExternalOutput").ap()
        p.y(lambda h: h.dma_start(out=dbg_gw[:, :], in_=gwT[:]))
        dbg_g3 = nc.dram_tensor("dbg_g3", [128, 192], F32,
                                kind="ExternalOutput").ap()
        p.y(lambda h: h.dma_start(out=dbg_g3[:, :], in_=stats3_sb[:]))
    p.barrier()
    p1s.close()

    if phases <= 1:
        finish_early()
        xnkv_ctx.close(); xnq_ctx.close(); qhat_ctx.close()
        return _finalize(nc, p, sems, ctx)

    p2s = ExitStack()
    e2 = p2s.enter_context
    cosk_sb = e2(nc.sbuf_tensor("cosk_sb", [128, 32 * HD], F32))
    sink_sb = e2(nc.sbuf_tensor("sink_sb", [128, 32 * HD], F32))
    cosq_sb = e2(nc.sbuf_tensor("cosq_sb", [128, 16 * HD], F32))
    sinq_sb = e2(nc.sbuf_tensor("sinq_sb", [128, 16 * HD], F32))
    wTt = e2(nc.sbuf_tensor("wTt", [128, 8 * D], BF16))
    wnat = [e2(nc.sbuf_tensor(f"wnat{i}", [128, D], BF16)) for i in range(2)]
    rots = e2(nc.sbuf_tensor("rots", [128, 512], F32))
    t1s = e2(nc.sbuf_tensor("t1s", [128, 512], F32))
    t2s = e2(nc.sbuf_tensor("t2s", [128, 512], F32))
    khat = [e2(nc.sbuf_tensor(f"khat{i}", [128, 512], BF16)) for i in range(2)]
    stage = [e2(nc.sbuf_tensor(f"stage{i}", [128, D], BF16)) for i in range(2)]
    pw = [e2(nc.psum_tensor(f"pw{i}", [128, D], BF16)) for i in range(2)]
    pk = [e2(nc.psum_tensor(f"pk{i}", [128, 512], F32)) for i in range(2)]
    pkt = [e2(nc.psum_tensor(f"pkt{i}", [128, 512], BF16)) for i in range(2)]

    rope_ld = None
    if os.environ.get("KSKIP_TAB"):
        for sb, n in ((cosk_sb, 4), (sink_sb, 4), (cosq_sb, 2), (sinq_sb, 2)):
            rope_ld = p.y(lambda h, sb=sb: h.dma_start(
                out=sb[0:1, 0:128], in_=cosk[0:1, :].rearrange("a (b c) -> a (b c)") if False else cosk[0:2, 0:64].rearrange("a b -> (a b)").unsqueeze(0)))
    else:
        for quarter in range(4):
            for tab, sb in ((cosk, cosk_sb), (sink, sink_sb)):
                src = tab.rearrange("(tc pp) j -> pp tc j", pp=128)
                rope_ld = p.y(lambda h, q4=quarter, src=src, sb=sb: h.dma_start(
                    out=sb[:].rearrange("pp (tc j) -> pp tc j", j=HD)[:, q4 * 8:(q4 + 1) * 8, :],
                    in_=src[:, q4 * 8:(q4 + 1) * 8, :]))
        for half in range(2):
            for tab, sb in ((cosq8, cosq_sb), (sinq8, sinq_sb)):
                src = tab.rearrange("(tc pp) j -> pp tc j", pp=128)
                rope_ld = p.y(lambda h, q4=half, src=src, sb=sb: h.dma_start(
                    out=sb[:].rearrange("pp (tc j) -> pp tc j", j=HD)[:, q4 * 8:(q4 + 1) * 8, :],
                    in_=src[:, q4 * 8:(q4 + 1) * 8, :]))

    def rope_emit(pslot, is_q, tile_idx, out_bf, extra_w):
        ctab = cosq_sb if is_q else cosk_sb
        stab = sinq_sb if is_q else sink_sb
        csl = ctab[:, tile_idx * HD:(tile_idx + 1) * HD].unsqueeze(1).broadcast_to(
            [128, 8, HD])
        ssl = stab[:, tile_idx * HD:(tile_idx + 1) * HD].unsqueeze(1).broadcast_to(
            [128, 8, HD])
        pv = pslot[:].rearrange("pp (b d) -> pp b d", b=8)
        rv4 = rots[:].rearrange("pp (b two hh) -> pp b two hh", b=8, two=2)
        rv3 = rots[:].rearrange("pp (b d) -> pp b d", b=8)
        tv3 = t1s[:].rearrange("pp (b d) -> pp b d", b=8)
        uv3 = t2s[:].rearrange("pp (b d) -> pp b d", b=8)
        p.v(lambda h: h.tensor_scalar_mul(rv4[:, :, 0, :], pv[:, :, 1::2], -1.0),
            waits=extra_w)
        p.v(lambda h: h.tensor_copy(rv4[:, :, 1, :], pv[:, :, 0::2]))
        psum_done = p.v(lambda h: h.tensor_mul(tv3, pv, csl))
        p.v(lambda h: h.tensor_mul(uv3, rv3, ssl))
        done = p.v(lambda h: h.tensor_add(out_bf[:], t1s[:], t2s[:]))
        return psum_done, done

    def build_wT(Wsrc, prev_use):
        evs = []
        for oc in range(8):
            slot = oc % 2
            gld = p.g(lambda h, Wsrc=Wsrc, oc=oc, slot=slot: h.dma_start(
                out=wnat[slot][:], in_=Wsrc[oc * 128:(oc + 1) * 128, :]),
                waits=(evs[oc - 2] if oc >= 2 else prev_use,))
            ps = pw[slot]
            tk = None
            for ic in range(8):
                tk = p.t(lambda h, slot=slot, ps=ps, ic=ic: h.transpose(
                    ps[:, ic * 128:(ic + 1) * 128],
                    wnat[slot][:, ic * 128:(ic + 1) * 128], idb[:]),
                    waits=(gld, evs[oc - 2] if oc >= 2 else None,
                           prev_use if oc < 2 else None, y_init) if ic == 0 else ())
            dv = wTt[:].rearrange("pp (ic o) -> pp ic o", ic=8)
            pv = ps[:].rearrange("pp (ic o) -> pp ic o", ic=8)
            evs.append(p.v(lambda h, dv=dv, pv=pv, oc=oc: h.tensor_copy(
                dv[:, :, oc * 128:(oc + 1) * 128], pv), waits=(tk,)))
        return evs[-1]

    prev_wT_use = None
    pk_free = [None, None]
    khat_free = [None, None]
    pkt_free = [None, None]
    rope_free = None
    stage_free = [None, None]
    for name in ("k", "q", "v"):
        Wsrc = {"k": Wk, "q": Wq, "v": Wv}[name]
        wT_ready = build_wT(Wsrc, prev_wT_use)
        ntile = 16 if name == "q" else 32
        if os.environ.get("KP2_NTILE"):
            ntile = min(ntile, int(os.environ["KP2_NTILE"]))
        if os.environ.get("KP2_PASSES") and name not in os.environ["KP2_PASSES"]:
            ntile = 0
        last_mm = None
        for tt in range(ntile):
            is_q_half = tt < 16
            xn_src = xnq if is_q_half else xnkv
            toff = (tt if is_q_half else tt - 16) * 128
            sslot = tt % 2
            for oh in range(2):
                pslot = pk[oh]
                tk = None
                for ic in range(8):
                    tk = p.t(lambda h, xn_src=xn_src, toff=toff, ic=ic, oh=oh,
                             pslot=pslot: h.matmul(
                        pslot[:],
                        xn_src[:, ic * Q + toff:ic * Q + toff + 128],
                        wTt[:, ic * D + oh * 512:ic * D + oh * 512 + 512],
                        start=(ic == 0), stop=(ic == 7)),
                        waits=(wT_ready, pk_free[oh], rope_ld) if ic == 0 else ())
                last_mm = tk
                if name == "v":
                    ev = p.v(lambda h, pslot=pslot, oh=oh, sslot=sslot: h.tensor_copy(
                        stage[sslot][:, oh * 512:oh * 512 + 512], pslot[:]),
                        waits=(tk, stage_free[sslot] if oh == 0 else None))
                    pk_free[oh] = ev
                    if oh == 1:
                        st = p.y(lambda h, sslot=sslot, tt=tt: h.dma_start(
                            out=V_s[tt, :, :], in_=stage[sslot][:]), waits=(ev,))
                        stage_free[sslot] = st
                else:
                    kslot = oh
                    kb = khat[kslot]
                    psum_done, rope_done = rope_emit(
                        pslot, name == "q", tt, kb,
                        (tk, khat_free[kslot], rope_free))
                    pk_free[oh] = psum_done
                    rope_free = rope_done
                    ttk = None
                    for j in range(4):
                        ttk = p.t(lambda h, kb=kb, j=j, kslot=kslot: h.transpose(
                            pkt[kslot][:, j * 128:(j + 1) * 128],
                            kb[:, j * 128:(j + 1) * 128], idb[:]),
                            waits=(rope_done, pkt_free[kslot]) if j == 0 else ())
                    khat_free[kslot] = ttk
                    if name == "k":
                        ev = p.v(lambda h, kslot=kslot, oh=oh, sslot=sslot: h.tensor_copy(
                            stage[sslot][:, oh * 512:oh * 512 + 512], pkt[kslot][:]),
                            waits=(ttk, stage_free[sslot] if oh == 0 else None))
                        pkt_free[kslot] = ev
                        if oh == 1:
                            glob = toff + (0 if is_q_half else Q)
                            if os.environ.get("KSKIP_KT"):
                                st = p.y(lambda h, sslot=sslot: h.dma_start(
                                    out=V_s[0, :, :], in_=stage[sslot][:]),
                                    waits=(ev,))
                            else:
                                dst = KT_s[:, :, glob:glob + 128].transpose([1, 0, 2])
                                src = stage[sslot][:].rearrange("pp (dc t) -> pp dc t", dc=8)
                                st = p.y(lambda h, dst=dst, src=src: h.dma_start(
                                    out=dst, in_=src), waits=(ev,))
                            stage_free[sslot] = st
                    else:
                        ev = p.v(lambda h, kslot=kslot, oh=oh, sslot=sslot: h.tensor_copy(
                            stage[sslot][:, oh * 512:oh * 512 + 512], pkt[kslot][:]),
                            waits=(ttk, stage_free[sslot] if oh == 0 else None))
                        pkt_free[kslot] = ev
                        if oh == 1:
                            dst = QT_s[:].rearrange("pp (hp t) -> pp hp t", hp=8)
                            dst = dst[:, :, toff:toff + 128]
                            src = stage[sslot][:].rearrange("pp (dc t) -> pp dc t", dc=8)
                            st = p.y(lambda h, dst=dst, src=src: h.dma_start(
                                out=dst, in_=src), waits=(ev,))
                            stage_free[sslot] = st
        prev_wT_use = last_mm

    if _TAPS:
        dbg_xn = nc.dram_tensor("dbg_xn", [2, 128, 8 * Q], BF16,
                                kind="ExternalOutput").ap()
        dbg_wt = nc.dram_tensor("dbg_wt", [128, 8 * D], BF16,
                                kind="ExternalOutput").ap()
        p.barrier()
        p.y(lambda h: h.dma_start(out=dbg_xn[0, :, :], in_=xnq[:]))
        p.y(lambda h: h.dma_start(out=dbg_xn[1, :, :], in_=xnkv[:]))
        p.y(lambda h: h.dma_start(out=dbg_wt[:, :], in_=wTt[:]))
    p.barrier()
    p2s.close()
    xnkv_ctx.close()
    if phases <= 2:
        finish_early()
        xnq_ctx.close(); ytot_ctx.close()
        return _finalize(nc, p, sems, ctx)

    p3s = ExitStack()
    e3 = p3s.enter_context
    GMAX = 5
    wg_nat = [e3(nc.sbuf_tensor("wg_nat0", [128, GMAX * D], BF16))] * 2
    wu_nat = [e3(nc.sbuf_tensor("wu_nat0", [128, GMAX * D], BF16))] * 2
    wd_nat = [e3(nc.sbuf_tensor("wd_nat0", [128, GMAX * D], BF16))] * 2
    wgTb = [e3(nc.sbuf_tensor(f"wgTb{i}", [128, GMAX * D], BF16)) for i in range(2)]
    wuTb = [e3(nc.sbuf_tensor(f"wuTb{i}", [128, GMAX * D], BF16)) for i in range(2)]
    wdTb = [e3(nc.sbuf_tensor(f"wdTb{i}", [128, GMAX * D], BF16)) for i in range(2)]
    gsil = [e3(nc.sbuf_tensor(f"gsil{i}", [128, 256], F32)) for i in range(2)]
    hsb = [e3(nc.sbuf_tensor(f"hsb{i}", [128, 256], BF16)) for i in range(3)]
    pgp = [e3(nc.psum_tensor(f"pgp{i}", [128, 256], F32)) for i in range(2)]
    pu = e3(nc.psum_tensor("pu", [128, 256], F32))
    py_ = [e3(nc.psum_tensor(f"py{i}", [128, D], F32)) for i in range(2)]
    pwt = e3(nc.psum_tensor("pwt", [128, D], BF16))

    wdv = wdown.rearrange("e (dc pd) f -> e pd dc f", pd=128)
    fg_bounds = []
    f0 = 0
    for gsz in FGROUPS:
        fg_bounds.append((f0, gsz))
        f0 += gsz
    steps = [(ee, gi) for ee in range(E) for gi in range(len(FGROUPS))]

    def emit_loads(step_idx, wait0):
        ee, gi = steps[step_idx]
        fa, gsz = fg_bounds[gi]
        par = step_idx % 2
        last = None
        for fcl in range(gsz):
            fc = fa + fcl
            w = (wait0,) if fcl == 0 else ()
            p.g(lambda h, ee=ee, fc=fc, fcl=fcl, par=par: h.dma_start(
                out=wg_nat[par][:, fcl * D:(fcl + 1) * D],
                in_=wgate[ee, fc * 128:(fc + 1) * 128, :]), waits=w)
            p.g(lambda h, ee=ee, fc=fc, fcl=fcl, par=par: h.dma_start(
                out=wu_nat[par][:, fcl * D:(fcl + 1) * D],
                in_=wup[ee, fc * 128:(fc + 1) * 128, :]))
            src = wdv[ee, :, :, fc * 128:(fc + 1) * 128]
            dst = wd_nat[par][:, fcl * D:(fcl + 1) * D].rearrange(
                "pp (dc pf) -> pp dc pf", dc=8)
            last = p.g(lambda h, src=src, dst=dst: h.dma_start(out=dst, in_=src))
        return last

    def make_transpose_batches(step_idx, ld_tick, first_wait, prev_last_ev):
        ee, gi = steps[step_idx]
        fa, gsz = fg_bounds[gi]
        par = step_idx % 2
        batches = []
        state = {"prev_ev": prev_last_ev, "first": True, "last": None}

        def mk(fcl, which, nat, dstT):
            def go():
                w = (state["prev_ev"],)
                if state["first"]:
                    w = w + (ld_tick, first_wait)
                    state["first"] = False
                tk = None
                for ic in range(8):
                    tk = p.t(lambda h, ic=ic: h.transpose(
                        pwt[:, ic * 128:(ic + 1) * 128],
                        nat[par][:, fcl * D + ic * 128:fcl * D + (ic + 1) * 128],
                        idb[:]), waits=w if ic == 0 else ())
                ev = p.v(lambda h: h.tensor_copy(
                    dstT[par][:, fcl * D:(fcl + 1) * D], pwt[:]), waits=(tk,))
                state["prev_ev"] = ev
                state["last"] = ev
            return go

        for fcl in range(gsz):
            for which, (nat, dstT) in enumerate(((wg_nat, wgTb), (wu_nat, wuTb),
                                                 (wd_nat, wdTb))):
                batches.append(mk(fcl, which, nat, dstT))
        return batches, state

    ld_tick = emit_loads(0, None)
    batches, tstate = make_transpose_batches(0, ld_tick, None, None)
    for b in batches:
        b()
    T_ready = tstate["last"]
    pwt_last_ev = tstate["last"]

    hist_sil = []
    hist_hmul = []
    hist_down = []
    yev = {}
    pending_batches = []
    pending_state = None

    for step_idx, (ee, gi) in enumerate(steps):
        fa, gsz = fg_bounds[gi]
        par = step_idx % 2
        if step_idx + 1 < len(steps):
            ld_next = emit_loads(step_idx + 1, T_ready)
        for qs in range(8):
            for fcl in range(gsz):
                pgsel = len(hist_sil) % 2
                w0 = (T_ready,)
                if len(hist_sil) >= 2:
                    w0 = w0 + (hist_sil[-2],)
                tkg = None
                for ic in range(8):
                    tkg = p.t(lambda h, par=par, fcl=fcl, ic=ic, qs=qs, pgsel=pgsel:
                              h.matmul(
                        pgp[pgsel][:],
                        wgTb[par][:, (fcl * 8 + ic) * 128:(fcl * 8 + ic + 1) * 128],
                        xnq[:, ic * Q + qs * 256:ic * Q + qs * 256 + 256],
                        start=(ic == 0), stop=(ic == 7)),
                        waits=w0 if ic == 0 else ())
                w1 = (hist_hmul[-1],) if hist_hmul else ()
                tku = None
                for ic in range(8):
                    tku = p.t(lambda h, par=par, fcl=fcl, ic=ic, qs=qs: h.matmul(
                        pu[:],
                        wuTb[par][:, (fcl * 8 + ic) * 128:(fcl * 8 + ic + 1) * 128],
                        xnq[:, ic * Q + qs * 256:ic * Q + qs * 256 + 256],
                        start=(ic == 0), stop=(ic == 7)),
                        waits=w1 if ic == 0 else ())
                gslot = len(hist_sil) % 2
                st_ = p.s(lambda h, gslot=gslot, pgsel=pgsel: h.activation(
                    gsil[gslot][:], pgp[pgsel][:], AF.Silu),
                    waits=(tkg, hist_hmul[-2] if len(hist_hmul) >= 2 else None))
                hist_sil.append(st_)
                hslot = len(hist_hmul) % 3
                w2 = (st_, tku)
                if len(hist_down) >= 3:
                    w2 = w2 + (hist_down[-3],)
                vm = p.v(lambda h, hslot=hslot, gslot=gslot: h.tensor_mul(
                    hsb[hslot][:], gsil[gslot][:], pu[:]), waits=w2)
                hist_hmul.append(vm)
                dt = None
                for t2 in range(2):
                    for dh in range(2):
                        w3 = [vm] if (t2 == 0 and dh == 0) else []
                        if fcl == 0 and t2 == 0 and dh == 0:
                            w3.append(yev.get((qs - 1) % 8))
                        dt = p.t(lambda h, hslot=hslot, t2=t2, dh=dh, par=par,
                                 fcl=fcl, gsz=gsz: h.matmul(
                            py_[t2][:, dh * 512:dh * 512 + 512],
                            hsb[hslot][:, t2 * 128:t2 * 128 + 128],
                            wdTb[par][:, fcl * D + dh * 512:fcl * D + dh * 512 + 512],
                            start=(fcl == 0), stop=(fcl == gsz - 1)),
                            waits=tuple(w3))
                hist_down.append(dt)
                if pending_batches:
                    pending_batches.pop(0)()
            last_ev = None
            for t2 in range(2):
                qt = qs * 2 + t2
                cs = comb[:, qt * E + ee:qt * E + ee + 1]
                ysl = y_tot[:, qt * D:(qt + 1) * D]
                if step_idx == 0:
                    vt_ = p.v(lambda h, t2=t2, cs=cs, ysl=ysl: h.tensor_scalar(
                        ysl, py_[t2][:], cs, None, ALU.mult), waits=(dt,))
                else:
                    vt_ = p.v(lambda h, t2=t2, cs=cs, ysl=ysl: h.scalar_tensor_tensor(
                        ysl, py_[t2][:], cs, ysl, ALU.mult, ALU.add), waits=(dt,))
                last_ev = vt_
            yev[qs] = last_ev
            if qs == 2 and step_idx + 1 < len(steps):
                pending_batches, pending_state = make_transpose_batches(
                    step_idx + 1, ld_next, dt, pwt_last_ev)
        for b in pending_batches:
            b()
        pending_batches = []
        if pending_state is not None:
            T_ready = pending_state["last"]
            pwt_last_ev = pending_state["last"]
            pending_state = None

    p.barrier()
    p3s.close()
    xnq_ctx.close()
    if _TAPS:
        dbg_y = nc.dram_tensor("dbg_y", [128, 16 * D], F32, kind="ExternalOutput").ap()
        p.y(lambda h: h.dma_start(out=dbg_y[:, :], in_=y_tot[:]))
        p.barrier()
    if phases <= 3:
        finish_early()
        ytot_ctx.close()
        return _finalize(nc, p, sems, ctx)

    osb_ctx = ExitStack()
    osb = osb_ctx.enter_context(nc.sbuf_tensor("osb", [128, 8 * Q], BF16))

    if os.environ.get("KSYNC_P4"):
        p.sync_mode = True
    p4s = ExitStack()
    e4 = p4s.enter_context
    qhat = e4(nc.sbuf_tensor("qhat2", [128, 8 * Q], BF16))
    khp = [e4(nc.sbuf_tensor(f"khp{i}", [128, S], BF16)) for i in range(2)]
    vhp = [e4(nc.sbuf_tensor(f"vhp{i}", [128, 32 * 128], BF16)) for i in range(2)]
    expd = e4(nc.sbuf_tensor("expd", [128, 1024], BF16))
    red4 = e4(nc.sbuf_tensor("red4", [128, 4], F32))
    m5 = e4(nc.sbuf_tensor("m5", [128, 5], F32))
    nm4 = e4(nc.sbuf_tensor("nm4", [128, 4], F32))
    f4 = e4(nc.sbuf_tensor("f4", [128, 4], F32))
    z5 = e4(nc.sbuf_tensor("z5", [128, 5], F32))
    zt4 = e4(nc.sbuf_tensor("zt4", [128, 4], F32))
    zr16 = [e4(nc.sbuf_tensor(f"zr16_{i}", [128, 16], F32)) for i in range(2)]
    drow = [e4(nc.sbuf_tensor(f"drow{i}", [1, 512], F32)) for i in range(2)]
    mrow = [e4(nc.sbuf_tensor(f"mrow{i}", [1, 512], F32)) for i in range(2)]
    dbc = e4(nc.sbuf_tensor("dbc", [128, 512], F32))
    mbc = e4(nc.sbuf_tensor("mbc", [128, 1024], F32))
    subt = [e4(nc.sbuf_tensor(f"subt{i}", [128, 512], F32)) for i in range(4)]
    pts = [e4(nc.sbuf_tensor(f"pts{i}", [128, 512], BF16)) for i in range(4)]
    psA = [e4(nc.psum_tensor(f"psA{i}", [128, 1024], F32)) for i in range(4)]

    for half in range(2):
        qre = p.y(lambda h, half=half: h.dma_start(
            out=qhat[:, half * 8192:(half + 1) * 8192],
            in_=QT_s[:, half * 8192:(half + 1) * 8192]))
    p.emit("t", None, (qre,), 0)
    psA_free = [None] * 4
    kv_free = [None, None]
    pts_free = [None] * 4
    qc_prev_s = None
    qc_prev_v = None

    for hp in range(8):
        par = hp % 2
        kld = p.y(lambda h, hp=hp, par=par: h.dma_start(
            out=khp[par][:], in_=KT_s[hp, :, :]), waits=(kv_free[par],))
        vld = None
        for q4 in range(4):
            vsrc = V_s[q4 * 8:(q4 + 1) * 8, :, hp * 128:(hp + 1) * 128].transpose(
                [1, 0, 2])
            vld = p.y(lambda h, par=par, q4=q4, vsrc=vsrc: h.dma_start(
                out=vhp[par][:].rearrange("pp (tc t) -> pp tc t", tc=32)
                [:, q4 * 8:(q4 + 1) * 8, :], in_=vsrc))

        for head in range(2):
            hidx = hp * 2 + head
            r0 = head * 64
            for qc in range(16):
                def mm(ts):
                    mt_ = None
                    for half in range(2):
                        mt_ = p.t(lambda h, r0=r0, qc=qc, ts=ts, half=half, par=par,
                                  hp=hp: h.matmul(
                            psA[ts][:, half * 512:half * 512 + 512],
                            qhat[:, hp * Q + qc * 128:hp * Q + qc * 128 + 128]
                            [r0:r0 + 64, :],
                            khp[par][r0:r0 + 64,
                                     ts * 1024 + half * 512:ts * 1024 + half * 512 + 512],
                            start=True, stop=True),
                            waits=(kld, psA_free[ts]) if half == 0 else ())
                    return mt_

                def red(ts, mt_):
                    wv = (mt_,) + ((qc_prev_s,) if ts == 0 else ())
                    p.v(lambda h, ts=ts: h.reduce_max(
                        red4[:, ts:ts + 1], psA[ts][:], axis=AX.X), waits=wv)

                def mx(ts):
                    if ts == 0:
                        p.v(lambda h: h.tensor_copy(m5[:, 1:2], red4[:, 0:1]))
                    else:
                        p.v(lambda h, ts=ts: h.tensor_max(
                            m5[:, ts + 1:ts + 2], m5[:, ts:ts + 1],
                            red4[:, ts:ts + 1]))

                def vnm_se(ts):
                    vnm = p.v(lambda h, ts=ts: h.tensor_scalar_mul(
                        nm4[:, ts:ts + 1], m5[:, ts + 1:ts + 2], -1.0))
                    acc_dst = z5[:, 1:2] if ts == 0 else zt4[:, ts:ts + 1]
                    se = p.s(lambda h, acc_dst=acc_dst, ts=ts: h.activation(
                        expd[:], psA[ts][:], AF.Exp, bias=nm4[:, ts:ts + 1],
                        accum_out=acc_dst), waits=(vnm,))
                    psA_free[ts] = se
                    if ts > 0:
                        return p.s(lambda h, ts=ts: h.activation(
                            f4[:, ts:ts + 1], m5[:, ts:ts + 1], AF.Exp,
                            bias=nm4[:, ts:ts + 1]))
                    return se

                def stt(ts, sf):
                    p.v(lambda h, ts=ts: h.scalar_tensor_tensor(
                        z5[:, ts + 1:ts + 2], z5[:, ts:ts + 1], f4[:, ts:ts + 1],
                        zt4[:, ts:ts + 1], ALU.mult, ALU.add), waits=(sf,))

                red(0, mm(0))
                red(1, mm(1))
                mx(0)
                red(2, mm(2))
                s0 = vnm_se(0)
                mx(1)
                red(3, mm(3))
                sf1 = vnm_se(1)
                mx(2)
                stt(1, sf1)
                sf2 = vnm_se(2)
                mx(3)
                stt(2, sf2)
                sf3 = vnm_se(3)
                stt(3, sf3)
                qc_prev_s = sf3
                p.v(lambda h, hidx=hidx, qc=qc: h.tensor_copy(
                    m_all[:, hidx * 16 + qc:hidx * 16 + qc + 1], m5[:, 4:5]))
                p.v(lambda h, hidx=hidx, qc=qc: h.tensor_copy(
                    z_all[:, hidx * 16 + qc:hidx * 16 + qc + 1], z5[:, 4:5]))

        vzr = None
        for head in range(2):
            hidx = hp * 2 + head
            zsl = z_all[:, hidx * 16:(hidx + 1) * 16]
            vzr = p.v(lambda h, zsl=zsl, head=head: h.reciprocal(zr16[head][:], zsl),
                      waits=(qc_prev_s,) if head == 0 else ())

        psB = [psA[0], psA[1]]
        psO = psA[2]
        psB_free = {(0, 0): psA_free[0], (0, 1): psA_free[0],
                    (1, 0): psA_free[1], (1, 1): psA_free[1]}
        psO_free = psA_free[2]
        oev = None
        pv_tick = None
        subt_free = [None] * 4
        row_free = vzr
        for qs in range(4):
            tprev = None
            for head in range(2):
                hidx = hp * 2 + head
                for j in range(4):
                    qc = qs * 4 + j
                    regoff = (j % 2) * 128
                    msl = m_all[:, hidx * 16 + qc:hidx * 16 + qc + 1]
                    ttd = p.t(lambda h, regoff=regoff, msl=msl: h.transpose(
                        psA[3][0:1, regoff:regoff + 128], msl, idf[:]),
                        waits=(vzr, tprev, psA_free[3], row_free))
                    tprev = p.v(lambda h, head=head, j=j, regoff=regoff: h.tensor_copy(
                        mrow[head][0:1, j * 128:(j + 1) * 128],
                        psA[3][0:1, regoff:regoff + 128]), waits=(ttd,))
                for j in range(4):
                    qc = qs * 4 + j
                    regoff = (j % 2) * 128
                    ttd = p.t(lambda h, regoff=regoff, head=head, qc=qc: h.transpose(
                        psA[3][0:1, regoff:regoff + 128], zr16[head][:, qc:qc + 1],
                        idf[:]), waits=(tprev,))
                    tprev = p.v(lambda h, head=head, j=j, regoff=regoff: h.tensor_copy(
                        drow[head][0:1, j * 128:(j + 1) * 128],
                        psA[3][0:1, regoff:regoff + 128]), waits=(ttd,))
            rows_done = tprev
            tb = None
            for head in range(2):
                tb = p.t(lambda h, head=head: h.matmul(
                    psA[3][:, head * 512:head * 512 + 512], ones_sb[0:1, :],
                    mrow[head][0:1, :], start=True, stop=True),
                    waits=(rows_done,) if head == 0 else ())
            vmb = p.v(lambda h: h.tensor_copy(mbc[:], psA[3][:, 0:1024]), waits=(tb,))
            tb2 = None
            for head in range(2):
                tb2 = p.t(lambda h, head=head: h.matmul(
                    psA[3][head * 64:head * 64 + 64, 0:512], ones_sb[0:1, 0:64],
                    drow[head][0:1, :], start=True, stop=True,
                    tile_position=(0, head * 64)),
                    waits=(vmb,) if head == 0 else ())
            vdb = p.v(lambda h: h.tensor_copy(dbc[:], psA[3][:, 0:512]), waits=(tb2,))
            psA_free[3] = vdb
            row_free = tb2
            for tc in range(32):
                bhalf = tc % 2
                sx = None
                for head in range(2):
                    r0 = head * 64
                    mb = p.t(lambda h, head=head, r0=r0, tc=tc, qs=qs, par=par,
                             bhalf=bhalf, hp=hp: h.matmul(
                        psB[head][:, bhalf * 512:bhalf * 512 + 512],
                        khp[par][r0:r0 + 64, tc * 128:(tc + 1) * 128],
                        qhat[:, hp * Q + qs * 512:hp * Q + qs * 512 + 512]
                        [r0:r0 + 64, :],
                        start=True, stop=True),
                        waits=(psB_free[(head, bhalf)], vld))
                    slot = bhalf * 2 + head
                    vsub = p.v(lambda h, head=head, bhalf=bhalf, slot=slot: h.tensor_sub(
                        subt[slot][:], psB[head][:, bhalf * 512:bhalf * 512 + 512],
                        mbc[:, head * 512:head * 512 + 512]),
                        waits=(mb, vmb, subt_free[slot]))
                    psB_free[(head, bhalf)] = vsub
                    sx = p.s(lambda h, slot=slot: h.activation(
                        pts[slot][:], subt[slot][:], AF.Exp),
                        waits=(vsub, pts_free[slot]))
                    subt_free[slot] = sx
                wpv = (sx, psO_free if tc == 0 else None)
                for head in range(2):
                    slot = bhalf * 2 + head
                    pv_tick = p.t(lambda h, head=head, tc=tc, par=par, slot=slot:
                                  h.matmul(
                        psO[head * 64:head * 64 + 64, head * 512:head * 512 + 512],
                        vhp[par][:, tc * 128 + head * 64:tc * 128 + head * 64 + 64],
                        pts[slot][:], start=(tc == 0), stop=(tc == 31),
                        tile_position=(0, head * 64)),
                        waits=wpv if head == 0 else ())
                    pts_free[slot] = pv_tick
            p.v(lambda h, hp=hp, qs=qs: h.tensor_mul(
                osb[0:64, hp * Q + qs * 512:hp * Q + qs * 512 + 512],
                psO[0:64, 0:512], dbc[0:64, :]), waits=(pv_tick, vdb))
            oev = p.v(lambda h, hp=hp, qs=qs: h.tensor_mul(
                osb[64:128, hp * Q + qs * 512:hp * Q + qs * 512 + 512],
                psO[64:128, 512:1024], dbc[64:128, :]))
            psO_free = oev
        psA_free[0] = psB_free[(0, 1)]
        psA_free[1] = psB_free[(1, 1)]
        psA_free[2] = oev
        kv_free[par] = pv_tick

    if not os.environ.get("KSYNC_ALL"):
        p.sync_mode = False
    p.barrier()
    p4s.close()
    if _TAPS:
        dbg_m = nc.dram_tensor("dbg_m", [128, NH * 16], F32, kind="ExternalOutput").ap()
        dbg_z = nc.dram_tensor("dbg_z", [128, NH * 16], F32, kind="ExternalOutput").ap()
        dbg_o = nc.dram_tensor("dbg_o", [128, 8 * Q], BF16, kind="ExternalOutput").ap()
        p.y(lambda h: h.dma_start(out=dbg_m[:, :], in_=m_all[:]))
        p.y(lambda h: h.dma_start(out=dbg_z[:, :], in_=z_all[:]))
        p.y(lambda h: h.dma_start(out=dbg_o[:, :], in_=osb[:]))
        p.barrier()
    if phases <= 4:
        finish_early()
        osb_ctx.close(); ytot_ctx.close()
        return _finalize(nc, p, sems, ctx)

    p5s = ExitStack()
    e5 = p5s.enter_context
    woT = e5(nc.sbuf_tensor("woT", [128, 8 * D], BF16))
    wnat2 = [e5(nc.sbuf_tensor(f"wnat2{i}", [128, D], BF16)) for i in range(2)]
    xtile = [e5(nc.sbuf_tensor(f"xtile{i}", [128, D], F32)) for i in range(2)]
    gtile = [e5(nc.sbuf_tensor(f"gtile{i}", [128, D], F32)) for i in range(2)]
    otile = [e5(nc.sbuf_tensor(f"otile{i}", [128, D], F32)) for i in range(2)]
    pso = [e5(nc.psum_tensor(f"pso{i}", [128, 512], F32)) for i in range(2)]
    pw5 = e5(nc.psum_tensor("pw5", [128, D], BF16))

    wt5 = []
    for oc in range(8):
        slot = oc % 2
        gld = p.g(lambda h, oc=oc, slot=slot: h.dma_start(
            out=wnat2[slot][:], in_=Wo[oc * 128:(oc + 1) * 128, :]),
            waits=(wt5[oc - 2] if oc >= 2 else None,))
        tk = None
        for ic in range(8):
            tk = p.t(lambda h, slot=slot, ic=ic: h.transpose(
                pw5[:, ic * 128:(ic + 1) * 128],
                wnat2[slot][:, ic * 128:(ic + 1) * 128], idb[:]),
                waits=(gld, wt5[oc - 1] if oc >= 1 else None) if ic == 0 else ())
        dv = woT[:].rearrange("pp (ic o) -> pp ic o", ic=8)
        pv = pw5[:].rearrange("pp (ic o) -> pp ic o", ic=8)
        wt5.append(p.v(lambda h, dv=dv, pv=pv, oc=oc: h.tensor_copy(
            dv[:, :, oc * 128:(oc + 1) * 128], pv), waits=(tk,)))

    slot_rel = {}
    pso_free = [None, None]
    out_dma = {}
    sg_hist = {}
    for tokc in range(16):
        slot = tokc % 2
        xld = p.y(lambda h, tokc=tokc, slot=slot: h.dma_start(
            out=xtile[slot][:], in_=xi[tokc * 128:(tokc + 1) * 128, :]),
            waits=(slot_rel.get(tokc - 2),))
        sg = p.s(lambda h, slot=slot, tokc=tokc: h.activation(
            gtile[slot][:], y_tot[:, tokc * D:(tokc + 1) * D], AF.Silu),
            waits=(slot_rel.get(tokc - 2),))
        sg_hist[tokc] = sg
        vv2 = None
        for oh in range(2):
            tk = None
            for hpc in range(8):
                tk = p.t(lambda h, tokc=tokc, hpc=hpc, oh=oh: h.matmul(
                    pso[oh][:],
                    osb[:, hpc * Q + tokc * 128:hpc * Q + tokc * 128 + 128],
                    woT[:, hpc * D + oh * 512:hpc * D + oh * 512 + 512],
                    start=(hpc == 0), stop=(hpc == 7)),
                    waits=(wt5[-1], pso_free[oh]) if hpc == 0 else ())
            vv1 = p.v(lambda h, slot=slot, oh=oh: h.tensor_mul(
                otile[slot][:, oh * 512:oh * 512 + 512], pso[oh][:],
                gtile[slot][:, oh * 512:oh * 512 + 512]),
                waits=(tk, sg, xld, out_dma.get(tokc - 2)))
            pso_free[oh] = vv1
            vv2 = p.v(lambda h, slot=slot, oh=oh: h.tensor_add(
                otile[slot][:, oh * 512:oh * 512 + 512],
                otile[slot][:, oh * 512:oh * 512 + 512],
                xtile[slot][:, oh * 512:oh * 512 + 512]))
        slot_rel[tokc] = vv2
        out_dma[tokc] = p.y(lambda h, tokc=tokc, slot=slot: h.dma_start(
            out=out[tokc * 128:(tokc + 1) * 128, :], in_=otile[slot][:]),
            waits=(vv2,))

    p.emit("y", None, (out_dma[15],), 0)
    p5s.close()
    osb_ctx.close()
    ytot_ctx.close()

    return _finalize(nc, p, sems, ctx)


_CACHED = {}


def kernel(L0_attn_hidden_states, L0_attn_Wq, L0_attn_Wk, L0_attn_Wv, L0_attn_Wo,
           L0_ffn_gate_w, L0_ffn_w_gate, L0_ffn_w_up, L0_ffn_w_down):
    x = np.ascontiguousarray(L0_attn_hidden_states)
    B = x.shape[0]
    cos, sin = rope_tables()
    identb = np.eye(128, dtype=ml_dtypes.bfloat16)
    identf = np.eye(128, dtype=np.float32)
    onesf = np.ones((1, 128), dtype=np.float32)

    if "nc" not in _CACHED:
        _CACHED["nc"] = build_program()
    nc = _CACHED["nc"]

    shared = {
        "Wq": np.ascontiguousarray(L0_attn_Wq),
        "Wk": np.ascontiguousarray(L0_attn_Wk),
        "Wv": np.ascontiguousarray(L0_attn_Wv),
        "Wo": np.ascontiguousarray(L0_attn_Wo),
        "gwTin": np.ascontiguousarray(
            L0_ffn_gate_w.reshape(E, 8, 128).transpose(2, 1, 0).reshape(128, 8 * E)
            .astype(np.float32)),
        "w_gate": np.ascontiguousarray(L0_ffn_w_gate),
        "w_up": np.ascontiguousarray(L0_ffn_w_up),
        "w_down": np.ascontiguousarray(L0_ffn_w_down),
        "identb": identb, "identf": identf, "onesf": onesf,
    }
    in_maps = []
    for c in range(8):
        b, hh = c // 2, c % 2
        xq = x[b, hh * Q:(hh + 1) * Q, :]
        xo = x[b, (1 - hh) * Q:(2 - hh) * Q, :]
        ck = np.concatenate([cos[hh * Q:(hh + 1) * Q], cos[(1 - hh) * Q:(2 - hh) * Q]])
        sk = np.concatenate([sin[hh * Q:(hh + 1) * Q], sin[(1 - hh) * Q:(2 - hh) * Q]])
        m = dict(shared)
        m["x_full"] = np.ascontiguousarray(np.concatenate([xq, xo], axis=0))
        m["cosq8"] = np.ascontiguousarray(8.0 * cos[hh * Q:(hh + 1) * Q])
        m["sinq8"] = np.ascontiguousarray(8.0 * sin[hh * Q:(hh + 1) * Q])
        m["cosk"] = np.ascontiguousarray(ck)
        m["sink"] = np.ascontiguousarray(sk)
        in_maps.append(m)

    res = run_bass_kernel_spmd(nc, in_maps, core_ids=list(range(8)))
    outp = np.empty((B, S, D), dtype=np.float32)
    for c in range(8):
        b, hh = c // 2, c % 2
        outp[b, hh * Q:(hh + 1) * Q, :] = res.results[c]["out"]
    return outp


# version 08
# speedup vs baseline: 1.1005x; 1.1005x over previous
import os
from collections import defaultdict
from contextlib import ExitStack

import numpy as np
import ml_dtypes

import concourse.bass as bass
import concourse.mybir as mybir
from concourse.bass_utils import run_bass_kernel_spmd

F32 = mybir.dt.float32
BF16 = mybir.dt.bfloat16
AF = mybir.ActivationFunctionType
ALU = mybir.AluOpType
AX = mybir.AxisListType

D = 1024
S = 4096
Q = 2048
NH = 16
HD = 64
E = 8
F = 2816
FGROUPS = [5, 5, 4, 4, 4]
NEG_BIG = -1.0e30
EPS = 1e-5

ENGS = ("t", "v", "s", "y", "g")


class Prog:

    def __init__(self):
        self.q = {e: [] for e in ENGS}
        self.cnt = defaultdict(int)
        self.sync_mode = False
        self.phase = "init"

    def emit(self, eng, fn, waits=(), inc=0):
        if self.sync_mode and fn is not None:
            waits = tuple(waits) + tuple((o, self.cnt[o]) for o in ENGS if o != eng)
        ws = []
        for w in waits:
            if w is None:
                continue
            sn, c = w
            assert c <= self.cnt[sn], f"impossible wait {sn}>={c} (at {self.cnt[sn]})"
            if sn == eng and fn is not None:
                continue
            ws.append((sn, c))
        if inc:
            self.cnt[eng] += inc
        self.q[eng].append((fn, tuple(ws), inc, self.phase))
        return (eng, self.cnt[eng])

    def t(self, fn, waits=(), inc=1):
        return self.emit("t", fn, waits, inc)

    def v(self, fn, waits=(), inc=1):
        return self.emit("v", fn, waits, inc)

    def s(self, fn, waits=(), inc=1):
        return self.emit("s", fn, waits, inc)

    def y(self, fn, waits=(), inc=16):
        return self.emit("y", fn, waits, inc)

    def g(self, fn, waits=(), inc=16):
        return self.emit("g", fn, waits, inc)

    def barrier(self):
        marks = {}
        for e in ENGS:
            if e in ("y", "g"):
                marks[e] = (e, self.cnt[e])
            else:
                marks[e] = self.emit(e, lambda h: h.nop(), (), 1)
        for e in ENGS:
            self.emit(e, None, tuple(marks[o] for o in ENGS if o != e), 0)

    def replay(self, nc, sems):
        handles = {"t": nc.tensor, "v": nc.vector, "s": nc.scalar,
                   "y": nc.sync, "g": nc.gpsimd}
        waited = {e: set() for e in ENGS}
        for e in ENGS:
            for fn, waits, inc, ph in self.q[e]:
                for sn, c in waits:
                    waited[sn].add(c)

        def run_engine(eng):
            h = handles[eng]
            is_dma_q = eng in ("y", "g")
            cum = 0
            pending = 0
            cur_ph = None
            scope = None
            for fn, waits, inc, ph in self.q[eng]:
                if ph != cur_ph:
                    if scope is not None:
                        scope.__exit__(None, None, None)
                    scope = nc.named_scope(ph)
                    scope.__enter__()
                    cur_ph = ph
                for sn, c in waits:
                    h.wait_ge(sems[sn], c)
                if fn is None:
                    continue
                ins = fn(h)
                cum += inc
                pending += inc
                if pending and (is_dma_q or cum in waited[eng]):
                    ins.then_inc(sems[eng], pending)
                    pending = 0
            if scope is not None:
                scope.__exit__(None, None, None)
        return run_engine


def _finalize(nc, p, sems, ctx):
    with nc.Block() as block:
        run = p.replay(nc, sems)
        block.sync(lambda h: run("y"))
        block.tensor(lambda h: run("t"))
        block.vector(lambda h: run("v"))
        block.scalar(lambda h: run("s"))
        block.gpsimd(lambda h: run("g"))
    ctx.close()
    return nc


def rope_tables():
    base = np.float32(10000.0) * np.float32(S / 2048.0) ** np.float32(HD / (HD - 2))
    inv_freq = (1.0 / base ** (np.arange(0, HD, 2, dtype=np.float32) / np.float32(HD)
                               )).astype(np.float32)
    t = np.arange(S, dtype=np.float32)
    freqs = np.einsum("i,j->ij", t, inv_freq).astype(np.float32)
    emb = np.concatenate([freqs, freqs], axis=-1)
    return np.cos(emb).astype(np.float32), np.sin(emb).astype(np.float32)


def build_program(phases=5):
    nc = bass.Bass("TRN2", target_bir_lowering=False, detect_race_conditions=False)

    xi = nc.dram_tensor("x_full", [S, D], F32, kind="ExternalInput").ap()
    Wq = nc.dram_tensor("Wq", [D, D], F32, kind="ExternalInput").ap()
    Wk = nc.dram_tensor("Wk", [D, D], F32, kind="ExternalInput").ap()
    Wv = nc.dram_tensor("Wv", [D, D], F32, kind="ExternalInput").ap()
    Wo = nc.dram_tensor("Wo", [D, D], F32, kind="ExternalInput").ap()
    gwTin = nc.dram_tensor("gwTin", [128, 8 * E], F32, kind="ExternalInput").ap()
    wgate = nc.dram_tensor("w_gate", [E, F, D], F32, kind="ExternalInput").ap()
    wup = nc.dram_tensor("w_up", [E, F, D], F32, kind="ExternalInput").ap()
    wdown = nc.dram_tensor("w_down", [E, D, F], F32, kind="ExternalInput").ap()
    cosq8 = nc.dram_tensor("cosq8", [Q, HD], F32, kind="ExternalInput").ap()
    sinq8 = nc.dram_tensor("sinq8", [Q, HD], F32, kind="ExternalInput").ap()
    cosk = nc.dram_tensor("cosk", [S, HD], F32, kind="ExternalInput").ap()
    sink = nc.dram_tensor("sink", [S, HD], F32, kind="ExternalInput").ap()
    identb = nc.dram_tensor("identb", [128, 128], BF16, kind="ExternalInput").ap()
    identf = nc.dram_tensor("identf", [128, 128], F32, kind="ExternalInput").ap()
    onesf = nc.dram_tensor("onesf", [1, 128], F32, kind="ExternalInput").ap()
    out = nc.dram_tensor("out", [Q, D], F32, kind="ExternalOutput").ap()

    _TAPS = bool(os.environ.get("KDBG_TAPS"))
    _skind = "ExternalOutput" if _TAPS else "Internal"
    KT_s = nc.dram_tensor("KT_s", [8, 128, S], BF16, kind=_skind).ap()
    V_s = nc.dram_tensor("V_s", [S // 128, 128, D], BF16, kind=_skind).ap()
    QT_s = nc.dram_tensor("QT_s", [128, 8 * Q], BF16, kind=_skind).ap()

    p = Prog()
    if os.environ.get("KSYNC_ALL"):
        p.sync_mode = True
    ctx = ExitStack()
    e = ctx.enter_context
    sems = {n: e(nc.semaphore(n)) for n in ENGS}

    idb = e(nc.sbuf_tensor("idb", [128, 128], BF16))
    idf = e(nc.sbuf_tensor("idf", [128, 128], F32))
    ones_sb = e(nc.sbuf_tensor("ones_sb", [1, 128], F32))
    gwT = e(nc.sbuf_tensor("gwT", [128, 8 * E], F32))
    comb = e(nc.sbuf_tensor("comb", [128, 16 * E], F32))
    m_all = e(nc.sbuf_tensor("m_all", [128, NH * 16], F32))
    z_all = e(nc.sbuf_tensor("z_all", [128, NH * 16], F32))
    junk1 = e(nc.sbuf_tensor("junk1", [128, 1], F32))

    ytot_ctx = ExitStack()
    y_tot = ytot_ctx.enter_context(nc.sbuf_tensor("y_tot", [128, 16 * D], F32))
    xnq_ctx = ExitStack()
    xnq = xnq_ctx.enter_context(nc.sbuf_tensor("xnq", [128, 8 * Q], BF16))
    xnkv_ctx = ExitStack()
    xnkv = xnkv_ctx.enter_context(nc.sbuf_tensor("xnkv", [128, 8 * Q], BF16))

    p.v(lambda h: h.memset(junk1[:], 0.0))
    def finish_early():
        with nc.sbuf_tensor("zout", [128, D], F32) as zout:
            zv = p.v(lambda h: h.memset(zout[:], 0.0))
            last = None
            for tokc in range(16):
                last = p.y(lambda h, tokc=tokc: h.dma_start(
                    out=out[tokc * 128:(tokc + 1) * 128, :], in_=zout[:]),
                    waits=(zv,))
            p.emit("y", None, (last,), 0)

    p.y(lambda h: h.dma_start(out=idb[:], in_=identb[:, :]))
    p.y(lambda h: h.dma_start(out=idf[:], in_=identf[:, :]))
    p.y(lambda h: h.dma_start(out=ones_sb[:], in_=onesf[:, :]))
    y_init = ("y", p.cnt["y"])

    v_gw = p.y(lambda h: h.dma_start(out=gwT[:], in_=gwTin[:, :]))

    p.phase = "P1"
    p1s = ExitStack()
    e1 = p1s.enter_context
    xti = [e1(nc.sbuf_tensor(f"xt{i}", [128, D], F32)) for i in range(3)]
    xnt = [e1(nc.sbuf_tensor(f"xnt{i}", [128, D], F32)) for i in range(2)]
    sqscr = e1(nc.sbuf_tensor("sqscr", [128, D], F32))
    xnfscr = [e1(nc.sbuf_tensor(f"xnfscr{i}", [128, D], F32)) for i in range(2)]
    s1t = e1(nc.sbuf_tensor("s1t", [128, 1], F32))
    s2t = e1(nc.sbuf_tensor("s2t", [128, 1], F32))
    mt = e1(nc.sbuf_tensor("mt", [128, 1], F32))
    vt = e1(nc.sbuf_tensor("vt", [128, 1], F32))
    sdt = e1(nc.sbuf_tensor("sdt", [128, 1], F32))
    r0t = e1(nc.sbuf_tensor("r0t", [128, 1], F32))
    rat = e1(nc.sbuf_tensor("rat", [128, 1], F32))
    rbt = e1(nc.sbuf_tensor("rbt", [128, 1], F32))
    rt = e1(nc.sbuf_tensor("rt", [128, 1], F32))
    nbt = e1(nc.sbuf_tensor("nbt", [128, 1], F32))
    gst = e1(nc.sbuf_tensor("gst", [128, E], F32))
    m1t = e1(nc.sbuf_tensor("m1t", [128, 1], F32))
    m2t = e1(nc.sbuf_tensor("m2t", [128, 1], F32))
    mask1 = e1(nc.sbuf_tensor("mask1", [128, E], F32))
    mask2 = e1(nc.sbuf_tensor("mask2", [128, E], F32))
    gsmt = e1(nc.sbuf_tensor("gsmt", [128, E], F32))
    p1t = e1(nc.sbuf_tensor("p1t", [128, 1], F32))
    p2t = e1(nc.sbuf_tensor("p2t", [128, 1], F32))
    pxn = [e1(nc.psum_tensor(f"pxn{i}", [128, D], F32)) for i in range(2)]
    pgs = e1(nc.psum_tensor("pgs", [128, E], F32))
    stats_sb = e1(nc.sbuf_tensor("stats_sb", [128, 128], F32)) if _TAPS else None
    stats2_sb = e1(nc.sbuf_tensor("stats2_sb", [128, 320], F32)) if _TAPS else None
    stats3_sb = e1(nc.sbuf_tensor("stats3_sb", [128, 192], F32)) if _TAPS else None

    load_t, norm_t, s_last, xpose_t, evac_t, gate_t, comb_t = {}, {}, {}, {}, {}, {}, {}
    for tt in range(32):
        slot = tt % 3
        load_t[tt] = p.y(lambda h, tt=tt, slot=slot: h.dma_start(
            out=xti[slot][:], in_=xi[tt * 128:(tt + 1) * 128, :]),
            waits=(s_last.get(tt - 3), norm_t.get(tt - 3)))
        xt = xti[slot]
        p.v(lambda h, xt=xt: h.reduce_sum(s1t[:], xt[:], axis=AX.X),
            waits=(load_t[tt], s_last.get(tt - 1)))
        sq = p.s(lambda h, xt=xt: h.activation(sqscr[:], xt[:], AF.Square,
                                               accum_out=s2t[:]),
                 waits=(load_t[tt], comb_t.get(tt - 1) or evac_t.get(tt - 1)))
        p.v(lambda h: h.tensor_scalar(vt[:], s2t[:], 1.0 / D, EPS, ALU.mult,
                                      ALU.add), waits=(sq,))
        p.v(lambda h: h.tensor_scalar_mul(mt[:], s1t[:], 1.0 / D))
        p.v(lambda h: h.tensor_copy(junk1[:], s1t[:]))
        p.v(lambda h: h.tensor_mul(rat[:], mt[:], mt[:]))
        p.v(lambda h: h.tensor_copy(junk1[:], s1t[:]))
        vsd = p.v(lambda h: h.tensor_sub(vt[:], vt[:], rat[:]))
        ssd = p.s(lambda h: h.activation(sdt[:], vt[:], AF.Sqrt), waits=(vsd,))
        s_last[tt] = ssd
        p.v(lambda h: h.reciprocal(r0t[:], sdt[:]), waits=(ssd,))
        p.v(lambda h: h.tensor_copy(junk1[:], s1t[:]))
        p.v(lambda h: h.tensor_mul(rat[:], r0t[:], r0t[:]))
        p.v(lambda h: h.tensor_copy(junk1[:], s1t[:]))
        p.v(lambda h: h.tensor_mul(rbt[:], rat[:], vt[:]))
        p.v(lambda h: h.tensor_copy(junk1[:], s1t[:]))
        p.v(lambda h: h.tensor_scalar(rbt[:], rbt[:], -0.5, 1.5, ALU.mult, ALU.add))
        p.v(lambda h: h.tensor_copy(junk1[:], s1t[:]))
        p.v(lambda h: h.tensor_mul(rt[:], r0t[:], rbt[:]))
        p.v(lambda h: h.tensor_copy(junk1[:], s1t[:]))
        vnb = p.v(lambda h: h.scalar_tensor_tensor(nbt[:], mt[:], -1.0, rt[:],
                                                   ALU.mult, ALU.mult))
        xn = xnt[tt % 2]
        norm_t[tt] = p.v(lambda h, xt=xt, xn=xn: h.tensor_scalar(
            xn[:], xt[:], rt[:, 0:1], nbt[:, 0:1], ALU.mult, ALU.add),
            waits=(vnb, xpose_t.get(tt - 2)))
        if _TAPS:
            p.v(lambda h, tt=tt: h.tensor_copy(stats_sb[:, tt:tt + 1], rt[:]))
            p.v(lambda h, tt=tt: h.tensor_copy(stats_sb[:, 32 + tt:33 + tt], nbt[:]))
            p.v(lambda h, tt=tt: h.tensor_copy(stats_sb[:, 64 + tt:65 + tt], s1t[:]))
            p.v(lambda h, tt=tt: h.tensor_copy(stats_sb[:, 96 + tt:97 + tt], s2t[:]))
        px = pxn[tt % 2]
        tk = None
        for ic in range(8):
            tk = p.t(lambda h, xn=xn, px=px, ic=ic: h.transpose(
                px[:, ic * 128:(ic + 1) * 128], xn[:, ic * 128:(ic + 1) * 128], idf[:]),
                waits=(norm_t[tt], evac_t.get(tt - 2), y_init) if ic == 0 else ())
        xpose_t[tt] = tk
        dstq = xnq if tt < 16 else xnkv
        off = (tt if tt < 16 else tt - 16) * 128
        dview = dstq[:].rearrange("p (ic t) -> p ic t", ic=8)
        pview = px[:].rearrange("p (ic t) -> p ic t", ic=8)
        ev = p.v(lambda h, dview=dview, pview=pview, off=off: h.tensor_copy(
            dview[:, :, off:off + 128], pview), waits=(xpose_t[tt],))
        if tt < 16:
            xnf = xnfscr[tt % 2]
            ev = p.v(lambda h, xnf=xnf, px=px: h.tensor_copy(xnf[:], px[:]),
                     waits=(gate_t.get(tt - 2),))
        evac_t[tt] = ev
        if tt < 16:
            xnf = xnfscr[tt % 2]
            gtk = None
            for ic in range(8):
                gtk = p.t(lambda h, xnf=xnf, ic=ic: h.matmul(
                    pgs[:], xnf[:, ic * 128:(ic + 1) * 128],
                    gwT[:, ic * 8:(ic + 1) * 8], start=(ic == 0), stop=(ic == 7)),
                    waits=(evac_t[tt], v_gw, comb_t.get(tt - 1)) if ic == 0 else ())
            gate_t[tt] = gtk
            p.v(lambda h: h.tensor_copy(gst[:], pgs[:]), waits=(gtk,))
            p.v(lambda h: h.reduce_max(m1t[:], pgs[:], axis=AX.X))
            p.v(lambda h: h.tensor_copy(junk1[:], m1t[:]))
            p.v(lambda h: h.tensor_scalar(mask1[:], gst[:], m1t[:, 0:1], None, ALU.is_ge))
            p.v(lambda h: h.tensor_copy(junk1[:], m1t[:]))
            p.v(lambda h: h.scalar_tensor_tensor(gsmt[:], mask1[:], NEG_BIG, gst[:],
                                                 ALU.mult, ALU.add))
            p.v(lambda h: h.tensor_copy(junk1[:], m1t[:]))
            vm2 = p.v(lambda h: h.reduce_max(m2t[:], gsmt[:], axis=AX.X))
            p.v(lambda h: h.tensor_copy(junk1[:], m2t[:]))
            vmk2 = p.v(lambda h: h.tensor_scalar(mask2[:], gsmt[:], m2t[:, 0:1], None,
                                                 ALU.is_ge))
            p.s(lambda h: h.activation(p1t[:], m2t[:], AF.Sigmoid,
                                       bias=m1t[:, 0:1], scale=-1.0), waits=(vm2,))
            sp2 = p.s(lambda h: h.activation(p2t[:], m1t[:], AF.Sigmoid,
                                             bias=m2t[:, 0:1], scale=-1.0))
            s_last[tt] = sp2
            cs = comb[:, tt * E:(tt + 1) * E]
            p.v(lambda h, cs=cs: h.tensor_scalar(cs, mask1[:], p1t[:, 0:1], None,
                                                 ALU.mult), waits=(sp2, vmk2))
            p.v(lambda h: h.tensor_copy(junk1[:], m2t[:]))
            comb_t[tt] = p.v(lambda h, cs=cs: h.scalar_tensor_tensor(
                cs, mask2[:], p2t[:, 0:1], cs, ALU.mult, ALU.add))
            if _TAPS:
                c0 = tt * 20
                p.v(lambda h, c0=c0: h.tensor_copy(
                    stats2_sb[:, c0:c0 + 8], gst[:]))
                p.v(lambda h, c0=c0: h.tensor_copy(
                    stats2_sb[:, c0 + 8:c0 + 16], mask1[:]))
                p.v(lambda h, c0=c0: h.tensor_copy(
                    stats2_sb[:, c0 + 16:c0 + 17], m1t[:]))
                p.v(lambda h, c0=c0: h.tensor_copy(
                    stats2_sb[:, c0 + 17:c0 + 18], m2t[:]))
                p.v(lambda h, c0=c0: h.tensor_copy(
                    stats2_sb[:, c0 + 18:c0 + 19], p1t[:]))
                comb_t[tt] = p.v(lambda h, c0=c0: h.tensor_copy(
                    stats2_sb[:, c0 + 19:c0 + 20], p2t[:]))
                if tt in (0, 7, 15):
                    k3 = {0: 0, 7: 64, 15: 128}[tt]
                    comb_t[tt] = p.v(lambda h, k3=k3: h.tensor_copy(
                        stats3_sb[:, k3:k3 + 64], gwT[:]))

    if _TAPS:
        dbg_st = nc.dram_tensor("dbg_st", [128, 128], F32,
                                kind="ExternalOutput").ap()
        dbg_c = nc.dram_tensor("dbg_c", [128, 16 * E], F32,
                               kind="ExternalOutput").ap()
        p.barrier()
        p.y(lambda h: h.dma_start(out=dbg_st[:, :], in_=stats_sb[:]))
        p.y(lambda h: h.dma_start(out=dbg_c[:, :], in_=comb[:]))
        dbg_g2 = nc.dram_tensor("dbg_g2", [128, 320], F32,
                                kind="ExternalOutput").ap()
        p.y(lambda h: h.dma_start(out=dbg_g2[:, :], in_=stats2_sb[:]))
        dbg_gw = nc.dram_tensor("dbg_gw", [128, 8 * E], F32,
                                kind="ExternalOutput").ap()
        p.y(lambda h: h.dma_start(out=dbg_gw[:, :], in_=gwT[:]))
        dbg_g3 = nc.dram_tensor("dbg_g3", [128, 192], F32,
                                kind="ExternalOutput").ap()
        p.y(lambda h: h.dma_start(out=dbg_g3[:, :], in_=stats3_sb[:]))
    p.barrier()
    p1s.close()

    if phases <= 1:
        finish_early()
        xnkv_ctx.close(); xnq_ctx.close(); qhat_ctx.close()
        return _finalize(nc, p, sems, ctx)

    p.phase = "P2"
    p2s = ExitStack()
    e2 = p2s.enter_context
    cosk_sb = e2(nc.sbuf_tensor("cosk_sb", [128, 32 * HD], F32))
    sink_sb = e2(nc.sbuf_tensor("sink_sb", [128, 32 * HD], F32))
    cosq_sb = e2(nc.sbuf_tensor("cosq_sb", [128, 16 * HD], F32))
    sinq_sb = e2(nc.sbuf_tensor("sinq_sb", [128, 16 * HD], F32))
    wTt = e2(nc.sbuf_tensor("wTt", [128, 8 * D], BF16))
    wnat = [e2(nc.sbuf_tensor(f"wnat{i}", [128, D], BF16)) for i in range(2)]
    rots = e2(nc.sbuf_tensor("rots", [128, 512], F32))
    t1s = e2(nc.sbuf_tensor("t1s", [128, 512], F32))
    t2s = e2(nc.sbuf_tensor("t2s", [128, 512], F32))
    khat = [e2(nc.sbuf_tensor(f"khat{i}", [128, 512], BF16)) for i in range(2)]
    stage = [e2(nc.sbuf_tensor(f"stage{i}", [128, D], BF16)) for i in range(2)]
    pw = [e2(nc.psum_tensor(f"pw{i}", [128, D], BF16)) for i in range(2)]
    pk = [e2(nc.psum_tensor(f"pk{i}", [128, 512], F32)) for i in range(2)]
    pkt = [e2(nc.psum_tensor(f"pkt{i}", [128, 512], BF16)) for i in range(2)]

    rope_ld = None
    if os.environ.get("KSKIP_TAB"):
        for sb, n in ((cosk_sb, 4), (sink_sb, 4), (cosq_sb, 2), (sinq_sb, 2)):
            rope_ld = p.y(lambda h, sb=sb: h.dma_start(
                out=sb[0:1, 0:128], in_=cosk[0:1, :].rearrange("a (b c) -> a (b c)") if False else cosk[0:2, 0:64].rearrange("a b -> (a b)").unsqueeze(0)))
    else:
        for quarter in range(4):
            for tab, sb in ((cosk, cosk_sb), (sink, sink_sb)):
                src = tab.rearrange("(tc pp) j -> pp tc j", pp=128)
                rope_ld = p.y(lambda h, q4=quarter, src=src, sb=sb: h.dma_start(
                    out=sb[:].rearrange("pp (tc j) -> pp tc j", j=HD)[:, q4 * 8:(q4 + 1) * 8, :],
                    in_=src[:, q4 * 8:(q4 + 1) * 8, :]))
        for half in range(2):
            for tab, sb in ((cosq8, cosq_sb), (sinq8, sinq_sb)):
                src = tab.rearrange("(tc pp) j -> pp tc j", pp=128)
                rope_ld = p.y(lambda h, q4=half, src=src, sb=sb: h.dma_start(
                    out=sb[:].rearrange("pp (tc j) -> pp tc j", j=HD)[:, q4 * 8:(q4 + 1) * 8, :],
                    in_=src[:, q4 * 8:(q4 + 1) * 8, :]))

    def rope_emit(pslot, is_q, tile_idx, out_bf, extra_w):
        ctab = cosq_sb if is_q else cosk_sb
        stab = sinq_sb if is_q else sink_sb
        csl = ctab[:, tile_idx * HD:(tile_idx + 1) * HD].unsqueeze(1).broadcast_to(
            [128, 8, HD])
        ssl = stab[:, tile_idx * HD:(tile_idx + 1) * HD].unsqueeze(1).broadcast_to(
            [128, 8, HD])
        pv = pslot[:].rearrange("pp (b d) -> pp b d", b=8)
        rv4 = rots[:].rearrange("pp (b two hh) -> pp b two hh", b=8, two=2)
        rv3 = rots[:].rearrange("pp (b d) -> pp b d", b=8)
        tv3 = t1s[:].rearrange("pp (b d) -> pp b d", b=8)
        uv3 = t2s[:].rearrange("pp (b d) -> pp b d", b=8)
        p.v(lambda h: h.tensor_scalar_mul(rv4[:, :, 0, :], pv[:, :, 1::2], -1.0),
            waits=extra_w)
        p.v(lambda h: h.tensor_copy(rv4[:, :, 1, :], pv[:, :, 0::2]))
        psum_done = p.v(lambda h: h.tensor_mul(tv3, pv, csl))
        p.v(lambda h: h.tensor_mul(uv3, rv3, ssl))
        done = p.v(lambda h: h.tensor_add(out_bf[:], t1s[:], t2s[:]))
        return psum_done, done

    def build_wT(Wsrc, prev_use):
        evs = []
        for oc in range(8):
            slot = oc % 2
            gld = p.g(lambda h, Wsrc=Wsrc, oc=oc, slot=slot: h.dma_start(
                out=wnat[slot][:], in_=Wsrc[oc * 128:(oc + 1) * 128, :]),
                waits=(evs[oc - 2] if oc >= 2 else prev_use,))
            ps = pw[slot]
            tk = None
            for ic in range(8):
                tk = p.t(lambda h, slot=slot, ps=ps, ic=ic: h.transpose(
                    ps[:, ic * 128:(ic + 1) * 128],
                    wnat[slot][:, ic * 128:(ic + 1) * 128], idb[:]),
                    waits=(gld, evs[oc - 2] if oc >= 2 else None,
                           prev_use if oc < 2 else None, y_init) if ic == 0 else ())
            dv = wTt[:].rearrange("pp (ic o) -> pp ic o", ic=8)
            pv = ps[:].rearrange("pp (ic o) -> pp ic o", ic=8)
            evs.append(p.v(lambda h, dv=dv, pv=pv, oc=oc: h.tensor_copy(
                dv[:, :, oc * 128:(oc + 1) * 128], pv), waits=(tk,)))
        return evs[-1]

    prev_wT_use = None
    pk_free = [None, None]
    khat_free = [None, None]
    pkt_free = [None, None]
    rope_free = None
    stage_free = [None, None]
    for name in ("k", "q", "v"):
        Wsrc = {"k": Wk, "q": Wq, "v": Wv}[name]
        wT_ready = build_wT(Wsrc, prev_wT_use)
        ntile = 16 if name == "q" else 32
        if os.environ.get("KP2_NTILE"):
            ntile = min(ntile, int(os.environ["KP2_NTILE"]))
        if os.environ.get("KP2_PASSES") and name not in os.environ["KP2_PASSES"]:
            ntile = 0
        last_mm = None
        for tt in range(ntile):
            is_q_half = tt < 16
            xn_src = xnq if is_q_half else xnkv
            toff = (tt if is_q_half else tt - 16) * 128
            sslot = tt % 2
            for oh in range(2):
                pslot = pk[oh]
                tk = None
                for ic in range(8):
                    tk = p.t(lambda h, xn_src=xn_src, toff=toff, ic=ic, oh=oh,
                             pslot=pslot: h.matmul(
                        pslot[:],
                        xn_src[:, ic * Q + toff:ic * Q + toff + 128],
                        wTt[:, ic * D + oh * 512:ic * D + oh * 512 + 512],
                        start=(ic == 0), stop=(ic == 7)),
                        waits=(wT_ready, pk_free[oh], rope_ld) if ic == 0 else ())
                last_mm = tk
                if name == "v":
                    ev = p.v(lambda h, pslot=pslot, oh=oh, sslot=sslot: h.tensor_copy(
                        stage[sslot][:, oh * 512:oh * 512 + 512], pslot[:]),
                        waits=(tk, stage_free[sslot] if oh == 0 else None))
                    pk_free[oh] = ev
                    if oh == 1:
                        st = p.y(lambda h, sslot=sslot, tt=tt: h.dma_start(
                            out=V_s[tt, :, :], in_=stage[sslot][:]), waits=(ev,))
                        stage_free[sslot] = st
                else:
                    kslot = oh
                    kb = khat[kslot]
                    psum_done, rope_done = rope_emit(
                        pslot, name == "q", tt, kb,
                        (tk, khat_free[kslot], rope_free))
                    pk_free[oh] = psum_done
                    rope_free = rope_done
                    ttk = None
                    for j in range(4):
                        ttk = p.t(lambda h, kb=kb, j=j, kslot=kslot: h.transpose(
                            pkt[kslot][:, j * 128:(j + 1) * 128],
                            kb[:, j * 128:(j + 1) * 128], idb[:]),
                            waits=(rope_done, pkt_free[kslot]) if j == 0 else ())
                    khat_free[kslot] = ttk
                    if name == "k":
                        ev = p.v(lambda h, kslot=kslot, oh=oh, sslot=sslot: h.tensor_copy(
                            stage[sslot][:, oh * 512:oh * 512 + 512], pkt[kslot][:]),
                            waits=(ttk, stage_free[sslot] if oh == 0 else None))
                        pkt_free[kslot] = ev
                        if oh == 1:
                            glob = toff + (0 if is_q_half else Q)
                            if os.environ.get("KSKIP_KT"):
                                st = p.y(lambda h, sslot=sslot: h.dma_start(
                                    out=V_s[0, :, :], in_=stage[sslot][:]),
                                    waits=(ev,))
                            else:
                                dst = KT_s[:, :, glob:glob + 128].transpose([1, 0, 2])
                                src = stage[sslot][:].rearrange("pp (dc t) -> pp dc t", dc=8)
                                st = p.y(lambda h, dst=dst, src=src: h.dma_start(
                                    out=dst, in_=src), waits=(ev,))
                            stage_free[sslot] = st
                    else:
                        ev = p.v(lambda h, kslot=kslot, oh=oh, sslot=sslot: h.tensor_copy(
                            stage[sslot][:, oh * 512:oh * 512 + 512], pkt[kslot][:]),
                            waits=(ttk, stage_free[sslot] if oh == 0 else None))
                        pkt_free[kslot] = ev
                        if oh == 1:
                            dst = QT_s[:].rearrange("pp (hp t) -> pp hp t", hp=8)
                            dst = dst[:, :, toff:toff + 128]
                            src = stage[sslot][:].rearrange("pp (dc t) -> pp dc t", dc=8)
                            st = p.y(lambda h, dst=dst, src=src: h.dma_start(
                                out=dst, in_=src), waits=(ev,))
                            stage_free[sslot] = st
        prev_wT_use = last_mm

    if _TAPS:
        dbg_xn = nc.dram_tensor("dbg_xn", [2, 128, 8 * Q], BF16,
                                kind="ExternalOutput").ap()
        dbg_wt = nc.dram_tensor("dbg_wt", [128, 8 * D], BF16,
                                kind="ExternalOutput").ap()
        p.barrier()
        p.y(lambda h: h.dma_start(out=dbg_xn[0, :, :], in_=xnq[:]))
        p.y(lambda h: h.dma_start(out=dbg_xn[1, :, :], in_=xnkv[:]))
        p.y(lambda h: h.dma_start(out=dbg_wt[:, :], in_=wTt[:]))
    p.barrier()
    p2s.close()
    xnkv_ctx.close()
    if phases <= 2:
        finish_early()
        xnq_ctx.close(); ytot_ctx.close()
        return _finalize(nc, p, sems, ctx)

    p.phase = "P3"
    p3s = ExitStack()
    e3 = p3s.enter_context
    GMAX = 5
    wg_nat = [e3(nc.sbuf_tensor("wg_nat0", [128, GMAX * D], BF16))] * 2
    wu_nat = [e3(nc.sbuf_tensor("wu_nat0", [128, GMAX * D], BF16))] * 2
    wd_nat = [e3(nc.sbuf_tensor("wd_nat0", [128, GMAX * D], BF16))] * 2
    wgTb = [e3(nc.sbuf_tensor(f"wgTb{i}", [128, GMAX * D], BF16)) for i in range(2)]
    wuTb = [e3(nc.sbuf_tensor(f"wuTb{i}", [128, GMAX * D], BF16)) for i in range(2)]
    wdTb = [e3(nc.sbuf_tensor(f"wdTb{i}", [128, GMAX * D], BF16)) for i in range(2)]
    gsil = [e3(nc.sbuf_tensor(f"gsil{i}", [128, 512], F32)) for i in range(2)]
    hsb = [e3(nc.sbuf_tensor(f"hsb{i}", [128, 512], BF16)) for i in range(GMAX)]
    pgp = e3(nc.psum_tensor("pgp", [128, 512], F32))
    pu = [e3(nc.psum_tensor(f"pu{i}", [128, 512], F32)) for i in range(2)]
    py_ = [e3(nc.psum_tensor(f"py{i}", [128, D], F32)) for i in range(2)]
    pwt = e3(nc.psum_tensor("pwt", [128, D], BF16))

    wdv = wdown.rearrange("e (dc pd) f -> e pd dc f", pd=128)
    fg_bounds = []
    f0 = 0
    for gsz in FGROUPS:
        fg_bounds.append((f0, gsz))
        f0 += gsz
    steps = [(ee, gi) for ee in range(E) for gi in range(len(FGROUPS))]

    def emit_loads(step_idx, wait0):
        ee, gi = steps[step_idx]
        fa, gsz = fg_bounds[gi]
        par = step_idx % 2
        last = None
        for fcl in range(gsz):
            fc = fa + fcl
            w = (wait0,) if fcl == 0 else ()
            p.g(lambda h, ee=ee, fc=fc, fcl=fcl, par=par: h.dma_start(
                out=wg_nat[par][:, fcl * D:(fcl + 1) * D],
                in_=wgate[ee, fc * 128:(fc + 1) * 128, :]), waits=w)
            p.g(lambda h, ee=ee, fc=fc, fcl=fcl, par=par: h.dma_start(
                out=wu_nat[par][:, fcl * D:(fcl + 1) * D],
                in_=wup[ee, fc * 128:(fc + 1) * 128, :]))
            src = wdv[ee, :, :, fc * 128:(fc + 1) * 128]
            dst = wd_nat[par][:, fcl * D:(fcl + 1) * D].rearrange(
                "pp (dc pf) -> pp dc pf", dc=8)
            last = p.g(lambda h, src=src, dst=dst: h.dma_start(out=dst, in_=src))
        return last

    def make_transpose_batches(step_idx, ld_tick, first_wait, prev_last_ev):
        ee, gi = steps[step_idx]
        fa, gsz = fg_bounds[gi]
        par = step_idx % 2
        batches = []
        state = {"prev_ev": prev_last_ev, "first": True, "last": None}

        def mk(fcl, which, nat, dstT):
            def go():
                w = (state["prev_ev"],)
                if state["first"]:
                    w = w + (ld_tick, first_wait)
                    state["first"] = False
                tk = None
                for ic in range(8):
                    tk = p.t(lambda h, ic=ic: h.transpose(
                        pwt[:, ic * 128:(ic + 1) * 128],
                        nat[par][:, fcl * D + ic * 128:fcl * D + (ic + 1) * 128],
                        idb[:]), waits=w if ic == 0 else ())
                ev = p.v(lambda h: h.tensor_copy(
                    dstT[par][:, fcl * D:(fcl + 1) * D], pwt[:]), waits=(tk,))
                state["prev_ev"] = ev
                state["last"] = ev
            return go

        for fcl in range(gsz):
            for which, (nat, dstT) in enumerate(((wg_nat, wgTb), (wu_nat, wuTb),
                                                 (wd_nat, wdTb))):
                batches.append(mk(fcl, which, nat, dstT))
        return batches, state

    ld_tick = emit_loads(0, None)
    batches, tstate = make_transpose_batches(0, ld_tick, None, None)
    for b in batches:
        b()
    T_ready = tstate["last"]
    pwt_last_ev = tstate["last"]

    hist_sil = []
    hist_hmul = []
    hist_down = []
    yev = {}
    pending_batches = []
    pending_state = None

    hs_last = [None] * GMAX
    pyfree = [None, None]
    for step_idx, (ee, gi) in enumerate(steps):
        fa, gsz = fg_bounds[gi]
        par = step_idx % 2
        if step_idx + 1 < len(steps):
            ld_next = emit_loads(step_idx + 1, T_ready)
        for qs2 in range(4):
            vm_fcl = [None] * gsz
            for fcl in range(gsz):
                w0 = (T_ready, hist_sil[-1] if hist_sil else None)
                tkg = None
                for ic in range(8):
                    tkg = p.t(lambda h, par=par, fcl=fcl, ic=ic, qs2=qs2: h.matmul(
                        pgp[:],
                        wgTb[par][:, (fcl * 8 + ic) * 128:(fcl * 8 + ic + 1) * 128],
                        xnq[:, ic * Q + qs2 * 512:ic * Q + qs2 * 512 + 512],
                        start=(ic == 0), stop=(ic == 7)),
                        waits=w0 if ic == 0 else ())
                pusel = len(hist_hmul) % 2
                w1 = (hist_hmul[-2],) if len(hist_hmul) >= 2 else ()
                tku = None
                for ic in range(8):
                    tku = p.t(lambda h, par=par, fcl=fcl, ic=ic, qs2=qs2,
                              pusel=pusel: h.matmul(
                        pu[pusel][:],
                        wuTb[par][:, (fcl * 8 + ic) * 128:(fcl * 8 + ic + 1) * 128],
                        xnq[:, ic * Q + qs2 * 512:ic * Q + qs2 * 512 + 512],
                        start=(ic == 0), stop=(ic == 7)),
                        waits=w1 if ic == 0 else ())
                gslot = len(hist_sil) % 2
                st_ = p.s(lambda h, gslot=gslot: h.activation(
                    gsil[gslot][:], pgp[:], AF.Silu),
                    waits=(tkg, hist_hmul[-2] if len(hist_hmul) >= 2 else None))
                hist_sil.append(st_)
                vm = p.v(lambda h, fcl=fcl, gslot=gslot, pusel=pusel: h.tensor_mul(
                    hsb[fcl][:], gsil[gslot][:], pu[pusel][:]),
                    waits=(st_, tku, hs_last[fcl]))
                hist_hmul.append(vm)
                vm_fcl[fcl] = vm
                if pending_batches:
                    pending_batches.pop(0)()
            for qh in range(2):
                dt = None
                for fcl in range(gsz):
                    for t2 in range(2):
                        for dh in range(2):
                            w3 = []
                            if t2 == 0 and dh == 0:
                                w3.append(vm_fcl[fcl])
                            if fcl == 0 and dh == 0:
                                w3.append(pyfree[t2])
                            dt = p.t(lambda h, fcl=fcl, t2=t2, dh=dh, par=par,
                                     qh=qh, gsz=gsz: h.matmul(
                                py_[t2][:, dh * 512:dh * 512 + 512],
                                hsb[fcl][:, qh * 256 + t2 * 128:
                                          qh * 256 + t2 * 128 + 128],
                                wdTb[par][:, fcl * D + dh * 512:
                                           fcl * D + dh * 512 + 512],
                                start=(fcl == 0), stop=(fcl == gsz - 1)),
                                waits=tuple(w3))
                    if qh == 1:
                        hs_last[fcl] = dt
                hist_down.append(dt)
                for t2 in range(2):
                    qt = qs2 * 4 + qh * 2 + t2
                    cs = comb[:, qt * E + ee:qt * E + ee + 1]
                    ysl = y_tot[:, qt * D:(qt + 1) * D]
                    if step_idx == 0:
                        vt_ = p.v(lambda h, t2=t2, cs=cs, ysl=ysl: h.tensor_scalar(
                            ysl, py_[t2][:], cs, None, ALU.mult), waits=(dt,))
                    else:
                        vt_ = p.v(lambda h, t2=t2, cs=cs, ysl=ysl:
                                  h.scalar_tensor_tensor(
                            ysl, py_[t2][:], cs, ysl, ALU.mult, ALU.add),
                            waits=(dt,))
                    pyfree[t2] = vt_
            if qs2 == 1 and step_idx + 1 < len(steps):
                pending_batches, pending_state = make_transpose_batches(
                    step_idx + 1, ld_next, dt, pwt_last_ev)
        for b in pending_batches:
            b()
        pending_batches = []
        if pending_state is not None:
            T_ready = pending_state["last"]
            pwt_last_ev = pending_state["last"]
            pending_state = None

    p.barrier()
    p3s.close()
    xnq_ctx.close()
    if _TAPS:
        dbg_y = nc.dram_tensor("dbg_y", [128, 16 * D], F32, kind="ExternalOutput").ap()
        p.y(lambda h: h.dma_start(out=dbg_y[:, :], in_=y_tot[:]))
        p.barrier()
    if phases <= 3:
        finish_early()
        ytot_ctx.close()
        return _finalize(nc, p, sems, ctx)

    osb_ctx = ExitStack()
    osb = osb_ctx.enter_context(nc.sbuf_tensor("osb", [128, 8 * Q], BF16))

    p.phase = "P4"
    if os.environ.get("KSYNC_P4"):
        p.sync_mode = True
    p4s = ExitStack()
    e4 = p4s.enter_context
    qhat = e4(nc.sbuf_tensor("qhat2", [128, 8 * Q], BF16))
    khp = [e4(nc.sbuf_tensor(f"khp{i}", [128, S], BF16)) for i in range(2)]
    vhp = [e4(nc.sbuf_tensor(f"vhp{i}", [128, 32 * 128], BF16)) for i in range(2)]
    expd = e4(nc.sbuf_tensor("expd", [128, 1024], BF16))
    red4 = e4(nc.sbuf_tensor("red4", [128, 4], F32))
    m5 = e4(nc.sbuf_tensor("m5", [128, 5], F32))
    nm4 = e4(nc.sbuf_tensor("nm4", [128, 4], F32))
    f4 = e4(nc.sbuf_tensor("f4", [128, 4], F32))
    drow = [e4(nc.sbuf_tensor(f"drow{i}", [1, 512], F32)) for i in range(2)]
    mrow = [e4(nc.sbuf_tensor(f"mrow{i}", [1, 512], F32)) for i in range(2)]
    dbc = e4(nc.sbuf_tensor("dbc", [128, 512], F32))
    mbc = e4(nc.sbuf_tensor("mbc", [128, 1024], F32))
    onesb = e4(nc.sbuf_tensor("onesb", [128, 1], BF16))
    subt = [e4(nc.sbuf_tensor(f"subt{i}", [128, 512], F32)) for i in range(4)]
    pts = [e4(nc.sbuf_tensor(f"pts{i}", [128, 512], BF16)) for i in range(4)]
    psA = [e4(nc.psum_tensor(f"psA{i}", [128, 1024], F32)) for i in range(4)]

    p.v(lambda h: h.memset(onesb[:], 1.0))
    for half in range(2):
        qre = p.y(lambda h, half=half: h.dma_start(
            out=qhat[:, half * 8192:(half + 1) * 8192],
            in_=QT_s[:, half * 8192:(half + 1) * 8192]))
    p.emit("t", None, (qre,), 0)
    psA_free = [None] * 4
    kv_free = [None, None]
    pts_free = [None] * 4
    qc_prev_s = None

    for hp in range(8):
        par = hp % 2
        kld = p.y(lambda h, hp=hp, par=par: h.dma_start(
            out=khp[par][:], in_=KT_s[hp, :, :]), waits=(kv_free[par],))
        vld = None
        for q4 in range(4):
            vsrc = V_s[q4 * 8:(q4 + 1) * 8, :, hp * 128:(hp + 1) * 128].transpose(
                [1, 0, 2])
            vld = p.y(lambda h, par=par, q4=q4, vsrc=vsrc: h.dma_start(
                out=vhp[par][:].rearrange("pp (tc t) -> pp tc t", tc=32)
                [:, q4 * 8:(q4 + 1) * 8, :], in_=vsrc))

        for head in range(2):
            hidx = hp * 2 + head
            r0 = head * 64
            for qc in range(16):
                def mm(ts):
                    mt_ = None
                    for half in range(2):
                        mt_ = p.t(lambda h, r0=r0, qc=qc, ts=ts, half=half, par=par,
                                  hp=hp: h.matmul(
                            psA[ts][:, half * 512:half * 512 + 512],
                            qhat[:, hp * Q + qc * 128:hp * Q + qc * 128 + 128]
                            [r0:r0 + 64, :],
                            khp[par][r0:r0 + 64,
                                     ts * 1024 + half * 512:ts * 1024 + half * 512 + 512],
                            start=True, stop=True),
                            waits=(kld, psA_free[ts]) if half == 0 else ())
                    return mt_

                def red(ts, mt_):
                    wv = (mt_,) + ((qc_prev_s,) if ts == 0 else ())
                    p.v(lambda h, ts=ts: h.reduce_max(
                        red4[:, ts:ts + 1], psA[ts][:], axis=AX.X), waits=wv)

                def mx(ts):
                    if ts == 0:
                        p.v(lambda h: h.tensor_copy(m5[:, 1:2], red4[:, 0:1]))
                    else:
                        p.v(lambda h, ts=ts: h.tensor_max(
                            m5[:, ts + 1:ts + 2], m5[:, ts:ts + 1],
                            red4[:, ts:ts + 1]))

                rtick = {}

                def red(ts, mt_):
                    rtick[ts] = p.v(lambda h, ts=ts: h.reduce_max(
                        red4[:, ts:ts + 1], psA[ts][:], axis=AX.X), waits=(mt_,))
                    psA_free[ts] = rtick[ts]

                red(0, mm(0))
                red(1, mm(1))
                mx(0)
                red(2, mm(2))
                mx(1)
                red(3, mm(3))
                mx(2)
                p.v(lambda h: h.tensor_copy(junk1[:], red4[:, 0:1]))
                mx(3)
                p.v(lambda h: h.tensor_copy(junk1[:], red4[:, 0:1]))
                vzr = p.v(lambda h, hidx=hidx, qc=qc: h.tensor_copy(
                    m_all[:, hidx * 16 + qc:hidx * 16 + qc + 1], m5[:, 4:5]))

        psB = [psA[0], psA[1]]
        psO = psA[2]
        psB_free = {(0, 0): psA_free[0], (0, 1): psA_free[0],
                    (1, 0): psA_free[1], (1, 1): psA_free[1]}
        psO_free = psA_free[2]
        oev = None
        pv_tick = None
        subt_free = [None] * 4
        row_free = vzr
        for qs in range(4):
            tprev = None
            for head in range(2):
                hidx = hp * 2 + head
                for j in range(4):
                    qc = qs * 4 + j
                    regoff = (j % 2) * 128
                    msl = m_all[:, hidx * 16 + qc:hidx * 16 + qc + 1]
                    ttd = p.t(lambda h, regoff=regoff, msl=msl: h.transpose(
                        psA[3][0:1, regoff:regoff + 128], msl, idf[:]),
                        waits=(vzr, tprev, psA_free[3], row_free))
                    tprev = p.v(lambda h, head=head, j=j, regoff=regoff: h.tensor_copy(
                        mrow[head][0:1, j * 128:(j + 1) * 128],
                        psA[3][0:1, regoff:regoff + 128]), waits=(ttd,))
            rows_done = tprev
            tb = None
            for head in range(2):
                tb = p.t(lambda h, head=head: h.matmul(
                    psA[3][:, head * 512:head * 512 + 512], ones_sb[0:1, :],
                    mrow[head][0:1, :], start=True, stop=True),
                    waits=(rows_done,) if head == 0 else ())
            vmb = p.v(lambda h: h.tensor_copy(mbc[:], psA[3][:, 0:1024]), waits=(tb,))
            z_tick = None
            for tc in range(32):
                bhalf = tc % 2
                sx = None
                for head in range(2):
                    r0 = head * 64
                    mb = p.t(lambda h, head=head, r0=r0, tc=tc, qs=qs, par=par,
                             bhalf=bhalf, hp=hp: h.matmul(
                        psB[head][:, bhalf * 512:bhalf * 512 + 512],
                        khp[par][r0:r0 + 64, tc * 128:(tc + 1) * 128],
                        qhat[:, hp * Q + qs * 512:hp * Q + qs * 512 + 512]
                        [r0:r0 + 64, :],
                        start=True, stop=True),
                        waits=(psB_free[(head, bhalf)], vld))
                    slot = bhalf * 2 + head
                    vsub = p.v(lambda h, head=head, bhalf=bhalf, slot=slot: h.tensor_sub(
                        subt[slot][:], psB[head][:, bhalf * 512:bhalf * 512 + 512],
                        mbc[:, head * 512:head * 512 + 512]),
                        waits=(mb, vmb, subt_free[slot]))
                    psB_free[(head, bhalf)] = vsub
                    sx = p.s(lambda h, slot=slot: h.activation(
                        pts[slot][:], subt[slot][:], AF.Exp),
                        waits=(vsub, pts_free[slot]))
                    subt_free[slot] = sx
                wpv = (sx, psO_free if tc == 0 else None)
                for head in range(2):
                    slot = bhalf * 2 + head
                    pv_tick = p.t(lambda h, head=head, tc=tc, par=par, slot=slot:
                                  h.matmul(
                        psO[head * 64:head * 64 + 64, head * 512:head * 512 + 512],
                        vhp[par][:, tc * 128 + head * 64:tc * 128 + head * 64 + 64],
                        pts[slot][:], start=(tc == 0), stop=(tc == 31),
                        tile_position=(0, head * 64)),
                        waits=wpv if head == 0 else ())
                for head in range(2):
                    slot = bhalf * 2 + head
                    wz = (vmb,) if (tc == 0 and head == 0) else ()
                    z_tick = p.t(lambda h, head=head, tc=tc, slot=slot: h.matmul(
                        psA[3][0:1, head * 512:head * 512 + 512],
                        onesb[:, 0:1], pts[slot][:],
                        start=(tc == 0), stop=(tc == 31)), waits=wz)
                    pts_free[slot] = z_tick
            zc0 = p.v(lambda h: h.tensor_copy(drow[0][0:1, :], psA[3][0:1, 0:512]),
                      waits=(z_tick,))
            p.v(lambda h: h.tensor_copy(drow[1][0:1, :], psA[3][0:1, 512:1024]))
            p.v(lambda h: h.reciprocal(drow[0][0:1, :], drow[0][0:1, :]))
            rc1 = p.v(lambda h: h.reciprocal(drow[1][0:1, :], drow[1][0:1, :]))
            tb2 = None
            for head in range(2):
                tb2 = p.t(lambda h, head=head: h.matmul(
                    psA[3][head * 64:head * 64 + 64, 0:512], ones_sb[0:1, 0:64],
                    drow[head][0:1, :], start=True, stop=True,
                    tile_position=(0, head * 64)),
                    waits=(rc1,) if head == 0 else ())
            vdb = p.v(lambda h: h.tensor_copy(dbc[:], psA[3][:, 0:512]), waits=(tb2,))
            psA_free[3] = vdb
            row_free = tb2
            p.v(lambda h, hp=hp, qs=qs: h.tensor_mul(
                osb[0:64, hp * Q + qs * 512:hp * Q + qs * 512 + 512],
                psO[0:64, 0:512], dbc[0:64, :]), waits=(pv_tick, vdb))
            oev = p.v(lambda h, hp=hp, qs=qs: h.tensor_mul(
                osb[64:128, hp * Q + qs * 512:hp * Q + qs * 512 + 512],
                psO[64:128, 512:1024], dbc[64:128, :]))
            psO_free = oev
        psA_free[0] = psB_free[(0, 1)]
        psA_free[1] = psB_free[(1, 1)]
        psA_free[2] = oev
        kv_free[par] = pv_tick

    if not os.environ.get("KSYNC_ALL"):
        p.sync_mode = False
    p.barrier()
    p4s.close()
    if _TAPS:
        dbg_m = nc.dram_tensor("dbg_m", [128, NH * 16], F32, kind="ExternalOutput").ap()
        dbg_z = nc.dram_tensor("dbg_z", [128, NH * 16], F32, kind="ExternalOutput").ap()
        dbg_o = nc.dram_tensor("dbg_o", [128, 8 * Q], BF16, kind="ExternalOutput").ap()
        p.y(lambda h: h.dma_start(out=dbg_m[:, :], in_=m_all[:]))
        p.y(lambda h: h.dma_start(out=dbg_z[:, :], in_=z_all[:]))
        p.y(lambda h: h.dma_start(out=dbg_o[:, :], in_=osb[:]))
        p.barrier()
    if phases <= 4:
        finish_early()
        osb_ctx.close(); ytot_ctx.close()
        return _finalize(nc, p, sems, ctx)

    p.phase = "P5"
    p5s = ExitStack()
    e5 = p5s.enter_context
    woT = e5(nc.sbuf_tensor("woT", [128, 8 * D], BF16))
    wnat2 = [e5(nc.sbuf_tensor(f"wnat2{i}", [128, D], BF16)) for i in range(2)]
    xtile = [e5(nc.sbuf_tensor(f"xtile{i}", [128, D], F32)) for i in range(2)]
    gtile = [e5(nc.sbuf_tensor(f"gtile{i}", [128, D], F32)) for i in range(2)]
    otile = [e5(nc.sbuf_tensor(f"otile{i}", [128, D], F32)) for i in range(2)]
    pso = [e5(nc.psum_tensor(f"pso{i}", [128, 512], F32)) for i in range(2)]
    pw5 = e5(nc.psum_tensor("pw5", [128, D], BF16))

    wt5 = []
    for oc in range(8):
        slot = oc % 2
        gld = p.g(lambda h, oc=oc, slot=slot: h.dma_start(
            out=wnat2[slot][:], in_=Wo[oc * 128:(oc + 1) * 128, :]),
            waits=(wt5[oc - 2] if oc >= 2 else None,))
        tk = None
        for ic in range(8):
            tk = p.t(lambda h, slot=slot, ic=ic: h.transpose(
                pw5[:, ic * 128:(ic + 1) * 128],
                wnat2[slot][:, ic * 128:(ic + 1) * 128], idb[:]),
                waits=(gld, wt5[oc - 1] if oc >= 1 else None) if ic == 0 else ())
        dv = woT[:].rearrange("pp (ic o) -> pp ic o", ic=8)
        pv = pw5[:].rearrange("pp (ic o) -> pp ic o", ic=8)
        wt5.append(p.v(lambda h, dv=dv, pv=pv, oc=oc: h.tensor_copy(
            dv[:, :, oc * 128:(oc + 1) * 128], pv), waits=(tk,)))

    slot_rel = {}
    pso_free = [None, None]
    out_dma = {}
    sg_hist = {}
    for tokc in range(16):
        slot = tokc % 2
        xld = p.y(lambda h, tokc=tokc, slot=slot: h.dma_start(
            out=xtile[slot][:], in_=xi[tokc * 128:(tokc + 1) * 128, :]),
            waits=(slot_rel.get(tokc - 2),))
        sg = p.s(lambda h, slot=slot, tokc=tokc: h.activation(
            gtile[slot][:], y_tot[:, tokc * D:(tokc + 1) * D], AF.Silu),
            waits=(slot_rel.get(tokc - 2),))
        sg_hist[tokc] = sg
        vv2 = None
        for oh in range(2):
            tk = None
            for hpc in range(8):
                tk = p.t(lambda h, tokc=tokc, hpc=hpc, oh=oh: h.matmul(
                    pso[oh][:],
                    osb[:, hpc * Q + tokc * 128:hpc * Q + tokc * 128 + 128],
                    woT[:, hpc * D + oh * 512:hpc * D + oh * 512 + 512],
                    start=(hpc == 0), stop=(hpc == 7)),
                    waits=(wt5[-1], pso_free[oh]) if hpc == 0 else ())
            vv1 = p.v(lambda h, slot=slot, oh=oh: h.tensor_mul(
                otile[slot][:, oh * 512:oh * 512 + 512], pso[oh][:],
                gtile[slot][:, oh * 512:oh * 512 + 512]),
                waits=(tk, sg, xld, out_dma.get(tokc - 2)))
            pso_free[oh] = vv1
            vv2 = p.v(lambda h, slot=slot, oh=oh: h.tensor_add(
                otile[slot][:, oh * 512:oh * 512 + 512],
                otile[slot][:, oh * 512:oh * 512 + 512],
                xtile[slot][:, oh * 512:oh * 512 + 512]))
        slot_rel[tokc] = vv2
        out_dma[tokc] = p.y(lambda h, tokc=tokc, slot=slot: h.dma_start(
            out=out[tokc * 128:(tokc + 1) * 128, :], in_=otile[slot][:]),
            waits=(vv2,))

    p.emit("y", None, (out_dma[15],), 0)
    p5s.close()
    osb_ctx.close()
    ytot_ctx.close()

    return _finalize(nc, p, sems, ctx)


_CACHED = {}


def kernel(L0_attn_hidden_states, L0_attn_Wq, L0_attn_Wk, L0_attn_Wv, L0_attn_Wo,
           L0_ffn_gate_w, L0_ffn_w_gate, L0_ffn_w_up, L0_ffn_w_down):
    x = np.ascontiguousarray(L0_attn_hidden_states)
    B = x.shape[0]
    cos, sin = rope_tables()
    identb = np.eye(128, dtype=ml_dtypes.bfloat16)
    identf = np.eye(128, dtype=np.float32)
    onesf = np.ones((1, 128), dtype=np.float32)

    if "nc" not in _CACHED:
        _CACHED["nc"] = build_program()
    nc = _CACHED["nc"]

    shared = {
        "Wq": np.ascontiguousarray(L0_attn_Wq),
        "Wk": np.ascontiguousarray(L0_attn_Wk),
        "Wv": np.ascontiguousarray(L0_attn_Wv),
        "Wo": np.ascontiguousarray(L0_attn_Wo),
        "gwTin": np.ascontiguousarray(
            L0_ffn_gate_w.reshape(E, 8, 128).transpose(2, 1, 0).reshape(128, 8 * E)
            .astype(np.float32)),
        "w_gate": np.ascontiguousarray(L0_ffn_w_gate),
        "w_up": np.ascontiguousarray(L0_ffn_w_up),
        "w_down": np.ascontiguousarray(L0_ffn_w_down),
        "identb": identb, "identf": identf, "onesf": onesf,
    }
    in_maps = []
    for c in range(8):
        b, hh = c // 2, c % 2
        xq = x[b, hh * Q:(hh + 1) * Q, :]
        xo = x[b, (1 - hh) * Q:(2 - hh) * Q, :]
        ck = np.concatenate([cos[hh * Q:(hh + 1) * Q], cos[(1 - hh) * Q:(2 - hh) * Q]])
        sk = np.concatenate([sin[hh * Q:(hh + 1) * Q], sin[(1 - hh) * Q:(2 - hh) * Q]])
        m = dict(shared)
        m["x_full"] = np.ascontiguousarray(np.concatenate([xq, xo], axis=0))
        m["cosq8"] = np.ascontiguousarray(8.0 * cos[hh * Q:(hh + 1) * Q])
        m["sinq8"] = np.ascontiguousarray(8.0 * sin[hh * Q:(hh + 1) * Q])
        m["cosk"] = np.ascontiguousarray(ck)
        m["sink"] = np.ascontiguousarray(sk)
        in_maps.append(m)

    res = run_bass_kernel_spmd(nc, in_maps, core_ids=list(range(8)))
    outp = np.empty((B, S, D), dtype=np.float32)
    for c in range(8):
        b, hh = c // 2, c % 2
        outp[b, hh * Q:(hh + 1) * Q, :] = res.results[c]["out"]
    return outp


# version 09 (final)
# speedup vs baseline: 1.1864x; 1.0780x over previous
import os
from collections import defaultdict
from contextlib import ExitStack

import numpy as np
import ml_dtypes

import concourse.bass as bass
import concourse.mybir as mybir
from concourse.bass_utils import run_bass_kernel_spmd

F32 = mybir.dt.float32
BF16 = mybir.dt.bfloat16
AF = mybir.ActivationFunctionType
ALU = mybir.AluOpType
AX = mybir.AxisListType

D = 1024
S = 4096
Q = 2048
NH = 16
HD = 64
E = 8
F = 2816
FGROUPS = [5, 5, 4, 4, 4]
NEG_BIG = -1.0e30
EPS = 1e-5

ENGS = ("t", "v", "s", "y", "g")


class Prog:

    def __init__(self):
        self.q = {e: [] for e in ENGS}
        self.cnt = defaultdict(int)
        self.sync_mode = False
        self.phase = "init"

    def emit(self, eng, fn, waits=(), inc=0):
        if self.sync_mode and fn is not None:
            waits = tuple(waits) + tuple((o, self.cnt[o]) for o in ENGS if o != eng)
        ws = []
        for w in waits:
            if w is None:
                continue
            sn, c = w
            assert c <= self.cnt[sn], f"impossible wait {sn}>={c} (at {self.cnt[sn]})"
            if sn == eng and fn is not None:
                continue
            ws.append((sn, c))
        if inc:
            self.cnt[eng] += inc
        self.q[eng].append((fn, tuple(ws), inc, self.phase))
        return (eng, self.cnt[eng])

    def t(self, fn, waits=(), inc=1):
        return self.emit("t", fn, waits, inc)

    def v(self, fn, waits=(), inc=1):
        return self.emit("v", fn, waits, inc)

    def s(self, fn, waits=(), inc=1):
        return self.emit("s", fn, waits, inc)

    def y(self, fn, waits=(), inc=16):
        return self.emit("y", fn, waits, inc)

    def g(self, fn, waits=(), inc=16):
        return self.emit("g", fn, waits, inc)

    def barrier(self):
        marks = {}
        for e in ENGS:
            if e in ("y", "g"):
                marks[e] = (e, self.cnt[e])
            else:
                marks[e] = self.emit(e, lambda h: h.nop(), (), 1)
        for e in ENGS:
            self.emit(e, None, tuple(marks[o] for o in ENGS if o != e), 0)

    def replay(self, nc, sems):
        handles = {"t": nc.tensor, "v": nc.vector, "s": nc.scalar,
                   "y": nc.sync, "g": nc.gpsimd}
        waited = {e: set() for e in ENGS}
        for e in ENGS:
            for fn, waits, inc, ph in self.q[e]:
                for sn, c in waits:
                    waited[sn].add(c)

        def run_engine(eng):
            h = handles[eng]
            is_dma_q = eng in ("y", "g")
            cum = 0
            pending = 0
            cur_ph = None
            scope = None
            for fn, waits, inc, ph in self.q[eng]:
                if ph != cur_ph:
                    if scope is not None:
                        scope.__exit__(None, None, None)
                    scope = nc.named_scope(ph)
                    scope.__enter__()
                    cur_ph = ph
                for sn, c in waits:
                    h.wait_ge(sems[sn], c)
                if fn is None:
                    continue
                ins = fn(h)
                cum += inc
                pending += inc
                if pending and (is_dma_q or cum in waited[eng]):
                    ins.then_inc(sems[eng], pending)
                    pending = 0
            if scope is not None:
                scope.__exit__(None, None, None)
        return run_engine


def _finalize(nc, p, sems, ctx):
    with nc.Block() as block:
        run = p.replay(nc, sems)
        block.sync(lambda h: run("y"))
        block.tensor(lambda h: run("t"))
        block.vector(lambda h: run("v"))
        block.scalar(lambda h: run("s"))
        block.gpsimd(lambda h: run("g"))
    ctx.close()
    return nc


def rope_tables():
    base = np.float32(10000.0) * np.float32(S / 2048.0) ** np.float32(HD / (HD - 2))
    inv_freq = (1.0 / base ** (np.arange(0, HD, 2, dtype=np.float32) / np.float32(HD)
                               )).astype(np.float32)
    t = np.arange(S, dtype=np.float32)
    freqs = np.einsum("i,j->ij", t, inv_freq).astype(np.float32)
    emb = np.concatenate([freqs, freqs], axis=-1)
    return np.cos(emb).astype(np.float32), np.sin(emb).astype(np.float32)


def build_program(phases=5):
    nc = bass.Bass("TRN2", target_bir_lowering=False, detect_race_conditions=False)

    xi = nc.dram_tensor("x_full", [S, D], F32, kind="ExternalInput").ap()
    Wq = nc.dram_tensor("Wq", [D, D], F32, kind="ExternalInput").ap()
    Wk = nc.dram_tensor("Wk", [D, D], F32, kind="ExternalInput").ap()
    Wv = nc.dram_tensor("Wv", [D, D], F32, kind="ExternalInput").ap()
    Wo = nc.dram_tensor("Wo", [D, D], F32, kind="ExternalInput").ap()
    gwTin = nc.dram_tensor("gwTin", [128, 8 * E], F32, kind="ExternalInput").ap()
    wgate = nc.dram_tensor("w_gate", [E, F, D], F32, kind="ExternalInput").ap()
    wup = nc.dram_tensor("w_up", [E, F, D], F32, kind="ExternalInput").ap()
    wdown = nc.dram_tensor("w_down", [E, D, F], F32, kind="ExternalInput").ap()
    cosq8 = nc.dram_tensor("cosq8", [Q, HD], F32, kind="ExternalInput").ap()
    sinq8 = nc.dram_tensor("sinq8", [Q, HD], F32, kind="ExternalInput").ap()
    cosk = nc.dram_tensor("cosk", [S, HD], F32, kind="ExternalInput").ap()
    sink = nc.dram_tensor("sink", [S, HD], F32, kind="ExternalInput").ap()
    identb = nc.dram_tensor("identb", [128, 128], BF16, kind="ExternalInput").ap()
    identf = nc.dram_tensor("identf", [128, 128], F32, kind="ExternalInput").ap()
    onesf = nc.dram_tensor("onesf", [1, 128], F32, kind="ExternalInput").ap()
    out = nc.dram_tensor("out", [Q, D], F32, kind="ExternalOutput").ap()

    _TAPS = bool(os.environ.get("KDBG_TAPS"))
    _skind = "ExternalOutput" if _TAPS else "Internal"
    KT_s = nc.dram_tensor("KT_s", [8, 128, S], BF16, kind=_skind).ap()
    V_s = nc.dram_tensor("V_s", [S // 128, 128, D], BF16, kind=_skind).ap()
    QT_s = nc.dram_tensor("QT_s", [128, 8 * Q], BF16, kind=_skind).ap()

    p = Prog()
    if os.environ.get("KSYNC_ALL"):
        p.sync_mode = True
    ctx = ExitStack()
    e = ctx.enter_context
    sems = {n: e(nc.semaphore(n)) for n in ENGS}

    idb = e(nc.sbuf_tensor("idb", [128, 128], BF16))
    idf = e(nc.sbuf_tensor("idf", [128, 128], F32))
    ones_sb = e(nc.sbuf_tensor("ones_sb", [1, 128], F32))
    gwT = e(nc.sbuf_tensor("gwT", [128, 8 * E], F32))
    comb = e(nc.sbuf_tensor("comb", [128, 16 * E], F32))
    m_all = e(nc.sbuf_tensor("m_all", [128, NH * 16], F32))
    z_all = e(nc.sbuf_tensor("z_all", [128, NH * 16], F32))
    junk1 = e(nc.sbuf_tensor("junk1", [128, 1], F32))

    ytot_ctx = ExitStack()
    y_tot = ytot_ctx.enter_context(nc.sbuf_tensor("y_tot", [128, 16 * D], F32))
    xnq_ctx = ExitStack()
    xnq = xnq_ctx.enter_context(nc.sbuf_tensor("xnq", [128, 8 * Q], BF16))
    xnkv_ctx = ExitStack()
    xnkv = xnkv_ctx.enter_context(nc.sbuf_tensor("xnkv", [128, 8 * Q], BF16))

    p.v(lambda h: h.memset(junk1[:], 0.0))
    def finish_early():
        with nc.sbuf_tensor("zout", [128, D], F32) as zout:
            zv = p.v(lambda h: h.memset(zout[:], 0.0))
            last = None
            for tokc in range(16):
                last = p.y(lambda h, tokc=tokc: h.dma_start(
                    out=out[tokc * 128:(tokc + 1) * 128, :], in_=zout[:]),
                    waits=(zv,))
            p.emit("y", None, (last,), 0)

    p.y(lambda h: h.dma_start(out=idb[:], in_=identb[:, :]))
    p.y(lambda h: h.dma_start(out=idf[:], in_=identf[:, :]))
    p.y(lambda h: h.dma_start(out=ones_sb[:], in_=onesf[:, :]))
    y_init = ("y", p.cnt["y"])

    v_gw = p.y(lambda h: h.dma_start(out=gwT[:], in_=gwTin[:, :]))

    p.phase = "P1"
    p1s = ExitStack()
    e1 = p1s.enter_context
    xti = [e1(nc.sbuf_tensor(f"xt{i}", [128, D], F32)) for i in range(3)]
    xnt = [e1(nc.sbuf_tensor(f"xnt{i}", [128, D], F32)) for i in range(2)]
    sqscr = e1(nc.sbuf_tensor("sqscr", [128, D], F32))
    xnfscr = [e1(nc.sbuf_tensor(f"xnfscr{i}", [128, D], F32)) for i in range(2)]
    s1t = e1(nc.sbuf_tensor("s1t", [128, 1], F32))
    s2t = e1(nc.sbuf_tensor("s2t", [128, 1], F32))
    mt = e1(nc.sbuf_tensor("mt", [128, 1], F32))
    vt = e1(nc.sbuf_tensor("vt", [128, 1], F32))
    sdt = e1(nc.sbuf_tensor("sdt", [128, 1], F32))
    r0t = e1(nc.sbuf_tensor("r0t", [128, 1], F32))
    rat = e1(nc.sbuf_tensor("rat", [128, 1], F32))
    rbt = e1(nc.sbuf_tensor("rbt", [128, 1], F32))
    rt = e1(nc.sbuf_tensor("rt", [128, 1], F32))
    nbt = e1(nc.sbuf_tensor("nbt", [128, 1], F32))
    gst = e1(nc.sbuf_tensor("gst", [128, E], F32))
    m1t = e1(nc.sbuf_tensor("m1t", [128, 1], F32))
    m2t = e1(nc.sbuf_tensor("m2t", [128, 1], F32))
    mask1 = e1(nc.sbuf_tensor("mask1", [128, E], F32))
    mask2 = e1(nc.sbuf_tensor("mask2", [128, E], F32))
    gsmt = e1(nc.sbuf_tensor("gsmt", [128, E], F32))
    p1t = e1(nc.sbuf_tensor("p1t", [128, 1], F32))
    p2t = e1(nc.sbuf_tensor("p2t", [128, 1], F32))
    pxn = [e1(nc.psum_tensor(f"pxn{i}", [128, D], F32)) for i in range(2)]
    pgs = e1(nc.psum_tensor("pgs", [128, E], F32))
    stats_sb = e1(nc.sbuf_tensor("stats_sb", [128, 128], F32)) if _TAPS else None
    stats2_sb = e1(nc.sbuf_tensor("stats2_sb", [128, 320], F32)) if _TAPS else None
    stats3_sb = e1(nc.sbuf_tensor("stats3_sb", [128, 192], F32)) if _TAPS else None

    load_t, norm_t, s_last, xpose_t, evac_t, gate_t, comb_t = {}, {}, {}, {}, {}, {}, {}
    for tt in range(32):
        slot = tt % 3
        load_t[tt] = p.y(lambda h, tt=tt, slot=slot: h.dma_start(
            out=xti[slot][:], in_=xi[tt * 128:(tt + 1) * 128, :]),
            waits=(s_last.get(tt - 3), norm_t.get(tt - 3)))
        xt = xti[slot]
        p.v(lambda h, xt=xt: h.reduce_sum(s1t[:], xt[:], axis=AX.X),
            waits=(load_t[tt], s_last.get(tt - 1)))
        sq = p.s(lambda h, xt=xt: h.activation(sqscr[:], xt[:], AF.Square,
                                               accum_out=s2t[:]),
                 waits=(load_t[tt], comb_t.get(tt - 1) or evac_t.get(tt - 1)))
        p.v(lambda h: h.tensor_scalar(vt[:], s2t[:], 1.0 / D, EPS, ALU.mult,
                                      ALU.add), waits=(sq,))
        p.v(lambda h: h.tensor_scalar_mul(mt[:], s1t[:], 1.0 / D))
        p.v(lambda h: h.tensor_copy(junk1[:], s1t[:]))
        p.v(lambda h: h.tensor_mul(rat[:], mt[:], mt[:]))
        p.v(lambda h: h.tensor_copy(junk1[:], s1t[:]))
        vsd = p.v(lambda h: h.tensor_sub(vt[:], vt[:], rat[:]))
        ssd = p.s(lambda h: h.activation(sdt[:], vt[:], AF.Sqrt), waits=(vsd,))
        s_last[tt] = ssd
        p.v(lambda h: h.reciprocal(r0t[:], sdt[:]), waits=(ssd,))
        p.v(lambda h: h.tensor_copy(junk1[:], s1t[:]))
        p.v(lambda h: h.tensor_mul(rat[:], r0t[:], r0t[:]))
        p.v(lambda h: h.tensor_copy(junk1[:], s1t[:]))
        p.v(lambda h: h.tensor_mul(rbt[:], rat[:], vt[:]))
        p.v(lambda h: h.tensor_copy(junk1[:], s1t[:]))
        p.v(lambda h: h.tensor_scalar(rbt[:], rbt[:], -0.5, 1.5, ALU.mult, ALU.add))
        p.v(lambda h: h.tensor_copy(junk1[:], s1t[:]))
        p.v(lambda h: h.tensor_mul(rt[:], r0t[:], rbt[:]))
        p.v(lambda h: h.tensor_copy(junk1[:], s1t[:]))
        vnb = p.v(lambda h: h.scalar_tensor_tensor(nbt[:], mt[:], -1.0, rt[:],
                                                   ALU.mult, ALU.mult))
        xn = xnt[tt % 2]
        norm_t[tt] = p.v(lambda h, xt=xt, xn=xn: h.tensor_scalar(
            xn[:], xt[:], rt[:, 0:1], nbt[:, 0:1], ALU.mult, ALU.add),
            waits=(vnb, xpose_t.get(tt - 2)))
        if _TAPS:
            p.v(lambda h, tt=tt: h.tensor_copy(stats_sb[:, tt:tt + 1], rt[:]))
            p.v(lambda h, tt=tt: h.tensor_copy(stats_sb[:, 32 + tt:33 + tt], nbt[:]))
            p.v(lambda h, tt=tt: h.tensor_copy(stats_sb[:, 64 + tt:65 + tt], s1t[:]))
            p.v(lambda h, tt=tt: h.tensor_copy(stats_sb[:, 96 + tt:97 + tt], s2t[:]))
        px = pxn[tt % 2]
        tk = None
        for ic in range(8):
            tk = p.t(lambda h, xn=xn, px=px, ic=ic: h.transpose(
                px[:, ic * 128:(ic + 1) * 128], xn[:, ic * 128:(ic + 1) * 128], idf[:]),
                waits=(norm_t[tt], evac_t.get(tt - 2), y_init) if ic == 0 else ())
        xpose_t[tt] = tk
        dstq = xnq if tt < 16 else xnkv
        off = (tt if tt < 16 else tt - 16) * 128
        dview = dstq[:].rearrange("p (ic t) -> p ic t", ic=8)
        pview = px[:].rearrange("p (ic t) -> p ic t", ic=8)
        ev = p.v(lambda h, dview=dview, pview=pview, off=off: h.tensor_copy(
            dview[:, :, off:off + 128], pview), waits=(xpose_t[tt],))
        if tt < 16:
            xnf = xnfscr[tt % 2]
            ev = p.v(lambda h, xnf=xnf, px=px: h.tensor_copy(xnf[:], px[:]),
                     waits=(gate_t.get(tt - 2),))
        evac_t[tt] = ev
        if tt < 16:
            xnf = xnfscr[tt % 2]
            gtk = None
            for ic in range(8):
                gtk = p.t(lambda h, xnf=xnf, ic=ic: h.matmul(
                    pgs[:], xnf[:, ic * 128:(ic + 1) * 128],
                    gwT[:, ic * 8:(ic + 1) * 8], start=(ic == 0), stop=(ic == 7)),
                    waits=(evac_t[tt], v_gw, comb_t.get(tt - 1)) if ic == 0 else ())
            gate_t[tt] = gtk
            p.v(lambda h: h.tensor_copy(gst[:], pgs[:]), waits=(gtk,))
            p.v(lambda h: h.reduce_max(m1t[:], pgs[:], axis=AX.X))
            p.v(lambda h: h.tensor_copy(junk1[:], m1t[:]))
            p.v(lambda h: h.tensor_scalar(mask1[:], gst[:], m1t[:, 0:1], None, ALU.is_ge))
            p.v(lambda h: h.tensor_copy(junk1[:], m1t[:]))
            p.v(lambda h: h.scalar_tensor_tensor(gsmt[:], mask1[:], NEG_BIG, gst[:],
                                                 ALU.mult, ALU.add))
            p.v(lambda h: h.tensor_copy(junk1[:], m1t[:]))
            vm2 = p.v(lambda h: h.reduce_max(m2t[:], gsmt[:], axis=AX.X))
            p.v(lambda h: h.tensor_copy(junk1[:], m2t[:]))
            vmk2 = p.v(lambda h: h.tensor_scalar(mask2[:], gsmt[:], m2t[:, 0:1], None,
                                                 ALU.is_ge))
            p.s(lambda h: h.activation(p1t[:], m2t[:], AF.Sigmoid,
                                       bias=m1t[:, 0:1], scale=-1.0), waits=(vm2,))
            sp2 = p.s(lambda h: h.activation(p2t[:], m1t[:], AF.Sigmoid,
                                             bias=m2t[:, 0:1], scale=-1.0))
            s_last[tt] = sp2
            cs = comb[:, tt * E:(tt + 1) * E]
            p.v(lambda h, cs=cs: h.tensor_scalar(cs, mask1[:], p1t[:, 0:1], None,
                                                 ALU.mult), waits=(sp2, vmk2))
            p.v(lambda h: h.tensor_copy(junk1[:], m2t[:]))
            comb_t[tt] = p.v(lambda h, cs=cs: h.scalar_tensor_tensor(
                cs, mask2[:], p2t[:, 0:1], cs, ALU.mult, ALU.add))
            if _TAPS:
                c0 = tt * 20
                p.v(lambda h, c0=c0: h.tensor_copy(
                    stats2_sb[:, c0:c0 + 8], gst[:]))
                p.v(lambda h, c0=c0: h.tensor_copy(
                    stats2_sb[:, c0 + 8:c0 + 16], mask1[:]))
                p.v(lambda h, c0=c0: h.tensor_copy(
                    stats2_sb[:, c0 + 16:c0 + 17], m1t[:]))
                p.v(lambda h, c0=c0: h.tensor_copy(
                    stats2_sb[:, c0 + 17:c0 + 18], m2t[:]))
                p.v(lambda h, c0=c0: h.tensor_copy(
                    stats2_sb[:, c0 + 18:c0 + 19], p1t[:]))
                comb_t[tt] = p.v(lambda h, c0=c0: h.tensor_copy(
                    stats2_sb[:, c0 + 19:c0 + 20], p2t[:]))
                if tt in (0, 7, 15):
                    k3 = {0: 0, 7: 64, 15: 128}[tt]
                    comb_t[tt] = p.v(lambda h, k3=k3: h.tensor_copy(
                        stats3_sb[:, k3:k3 + 64], gwT[:]))

    if _TAPS:
        dbg_st = nc.dram_tensor("dbg_st", [128, 128], F32,
                                kind="ExternalOutput").ap()
        dbg_c = nc.dram_tensor("dbg_c", [128, 16 * E], F32,
                               kind="ExternalOutput").ap()
        p.barrier()
        p.y(lambda h: h.dma_start(out=dbg_st[:, :], in_=stats_sb[:]))
        p.y(lambda h: h.dma_start(out=dbg_c[:, :], in_=comb[:]))
        dbg_g2 = nc.dram_tensor("dbg_g2", [128, 320], F32,
                                kind="ExternalOutput").ap()
        p.y(lambda h: h.dma_start(out=dbg_g2[:, :], in_=stats2_sb[:]))
        dbg_gw = nc.dram_tensor("dbg_gw", [128, 8 * E], F32,
                                kind="ExternalOutput").ap()
        p.y(lambda h: h.dma_start(out=dbg_gw[:, :], in_=gwT[:]))
        dbg_g3 = nc.dram_tensor("dbg_g3", [128, 192], F32,
                                kind="ExternalOutput").ap()
        p.y(lambda h: h.dma_start(out=dbg_g3[:, :], in_=stats3_sb[:]))
    p.barrier()
    p1s.close()

    if phases <= 1:
        finish_early()
        xnkv_ctx.close(); xnq_ctx.close(); qhat_ctx.close()
        return _finalize(nc, p, sems, ctx)

    p.phase = "P2"
    p2s = ExitStack()
    e2 = p2s.enter_context
    cosk_sb = e2(nc.sbuf_tensor("cosk_sb", [128, 32 * HD], F32))
    sink_sb = e2(nc.sbuf_tensor("sink_sb", [128, 32 * HD], F32))
    cosq_sb = e2(nc.sbuf_tensor("cosq_sb", [128, 16 * HD], F32))
    sinq_sb = e2(nc.sbuf_tensor("sinq_sb", [128, 16 * HD], F32))
    wTt = e2(nc.sbuf_tensor("wTt", [128, 8 * D], BF16))
    wnat = [e2(nc.sbuf_tensor(f"wnat{i}", [128, D], BF16)) for i in range(2)]
    rots = e2(nc.sbuf_tensor("rots", [128, 512], F32))
    t1s = e2(nc.sbuf_tensor("t1s", [128, 512], F32))
    t2s = e2(nc.sbuf_tensor("t2s", [128, 512], F32))
    khat = [e2(nc.sbuf_tensor(f"khat{i}", [128, 512], BF16)) for i in range(2)]
    stage = [e2(nc.sbuf_tensor(f"stage{i}", [128, D], BF16)) for i in range(2)]
    pw = [e2(nc.psum_tensor(f"pw{i}", [128, D], BF16)) for i in range(2)]
    pk = [e2(nc.psum_tensor(f"pk{i}", [128, 512], F32)) for i in range(2)]
    pkt = [e2(nc.psum_tensor(f"pkt{i}", [128, 512], BF16)) for i in range(2)]

    rope_ld = None
    if os.environ.get("KSKIP_TAB"):
        for sb, n in ((cosk_sb, 4), (sink_sb, 4), (cosq_sb, 2), (sinq_sb, 2)):
            rope_ld = p.y(lambda h, sb=sb: h.dma_start(
                out=sb[0:1, 0:128], in_=cosk[0:1, :].rearrange("a (b c) -> a (b c)") if False else cosk[0:2, 0:64].rearrange("a b -> (a b)").unsqueeze(0)))
    else:
        for quarter in range(4):
            for tab, sb in ((cosk, cosk_sb), (sink, sink_sb)):
                src = tab.rearrange("(tc pp) j -> pp tc j", pp=128)
                rope_ld = p.y(lambda h, q4=quarter, src=src, sb=sb: h.dma_start(
                    out=sb[:].rearrange("pp (tc j) -> pp tc j", j=HD)[:, q4 * 8:(q4 + 1) * 8, :],
                    in_=src[:, q4 * 8:(q4 + 1) * 8, :]))
        for half in range(2):
            for tab, sb in ((cosq8, cosq_sb), (sinq8, sinq_sb)):
                src = tab.rearrange("(tc pp) j -> pp tc j", pp=128)
                rope_ld = p.y(lambda h, q4=half, src=src, sb=sb: h.dma_start(
                    out=sb[:].rearrange("pp (tc j) -> pp tc j", j=HD)[:, q4 * 8:(q4 + 1) * 8, :],
                    in_=src[:, q4 * 8:(q4 + 1) * 8, :]))

    def rope_emit(pslot, is_q, tile_idx, out_bf, extra_w):
        ctab = cosq_sb if is_q else cosk_sb
        stab = sinq_sb if is_q else sink_sb
        csl = ctab[:, tile_idx * HD:(tile_idx + 1) * HD].unsqueeze(1).broadcast_to(
            [128, 8, HD])
        ssl = stab[:, tile_idx * HD:(tile_idx + 1) * HD].unsqueeze(1).broadcast_to(
            [128, 8, HD])
        pv = pslot[:].rearrange("pp (b d) -> pp b d", b=8)
        rv4 = rots[:].rearrange("pp (b two hh) -> pp b two hh", b=8, two=2)
        rv3 = rots[:].rearrange("pp (b d) -> pp b d", b=8)
        tv3 = t1s[:].rearrange("pp (b d) -> pp b d", b=8)
        uv3 = t2s[:].rearrange("pp (b d) -> pp b d", b=8)
        p.v(lambda h: h.tensor_scalar_mul(rv4[:, :, 0, :], pv[:, :, 1::2], -1.0),
            waits=extra_w)
        p.v(lambda h: h.tensor_copy(rv4[:, :, 1, :], pv[:, :, 0::2]))
        psum_done = p.v(lambda h: h.tensor_mul(tv3, pv, csl))
        p.v(lambda h: h.tensor_mul(uv3, rv3, ssl))
        done = p.v(lambda h: h.tensor_add(out_bf[:], t1s[:], t2s[:]))
        return psum_done, done

    def build_wT(Wsrc, prev_use):
        evs = []
        for oc in range(8):
            slot = oc % 2
            gld = p.g(lambda h, Wsrc=Wsrc, oc=oc, slot=slot: h.dma_start(
                out=wnat[slot][:], in_=Wsrc[oc * 128:(oc + 1) * 128, :]),
                waits=(evs[oc - 2] if oc >= 2 else prev_use,))
            ps = pw[slot]
            tk = None
            for ic in range(8):
                tk = p.t(lambda h, slot=slot, ps=ps, ic=ic: h.transpose(
                    ps[:, ic * 128:(ic + 1) * 128],
                    wnat[slot][:, ic * 128:(ic + 1) * 128], idb[:]),
                    waits=(gld, evs[oc - 2] if oc >= 2 else None,
                           prev_use if oc < 2 else None, y_init) if ic == 0 else ())
            dv = wTt[:].rearrange("pp (ic o) -> pp ic o", ic=8)
            pv = ps[:].rearrange("pp (ic o) -> pp ic o", ic=8)
            evs.append(p.v(lambda h, dv=dv, pv=pv, oc=oc: h.tensor_copy(
                dv[:, :, oc * 128:(oc + 1) * 128], pv), waits=(tk,)))
        return evs[-1]

    prev_wT_use = None
    pk_free = [None, None]
    khat_free = [None, None]
    pkt_free = [None, None]
    rope_free = None
    stage_free = [None, None]
    for name in ("k", "q", "v"):
        Wsrc = {"k": Wk, "q": Wq, "v": Wv}[name]
        wT_ready = build_wT(Wsrc, prev_wT_use)
        ntile = 16 if name == "q" else 32
        if os.environ.get("KP2_NTILE"):
            ntile = min(ntile, int(os.environ["KP2_NTILE"]))
        if os.environ.get("KP2_PASSES") and name not in os.environ["KP2_PASSES"]:
            ntile = 0
        last_mm = None
        for tt in range(ntile):
            is_q_half = tt < 16
            xn_src = xnq if is_q_half else xnkv
            toff = (tt if is_q_half else tt - 16) * 128
            sslot = tt % 2
            for oh in range(2):
                pslot = pk[oh]
                tk = None
                for ic in range(8):
                    tk = p.t(lambda h, xn_src=xn_src, toff=toff, ic=ic, oh=oh,
                             pslot=pslot: h.matmul(
                        pslot[:],
                        xn_src[:, ic * Q + toff:ic * Q + toff + 128],
                        wTt[:, ic * D + oh * 512:ic * D + oh * 512 + 512],
                        start=(ic == 0), stop=(ic == 7)),
                        waits=(wT_ready, pk_free[oh], rope_ld) if ic == 0 else ())
                last_mm = tk
                if name == "v":
                    ev = p.v(lambda h, pslot=pslot, oh=oh, sslot=sslot: h.tensor_copy(
                        stage[sslot][:, oh * 512:oh * 512 + 512], pslot[:]),
                        waits=(tk, stage_free[sslot] if oh == 0 else None))
                    pk_free[oh] = ev
                    if oh == 1:
                        st = p.y(lambda h, sslot=sslot, tt=tt: h.dma_start(
                            out=V_s[tt, :, :], in_=stage[sslot][:]), waits=(ev,))
                        stage_free[sslot] = st
                else:
                    kslot = oh
                    kb = khat[kslot]
                    psum_done, rope_done = rope_emit(
                        pslot, name == "q", tt, kb,
                        (tk, khat_free[kslot], rope_free))
                    pk_free[oh] = psum_done
                    rope_free = rope_done
                    ttk = None
                    for j in range(4):
                        ttk = p.t(lambda h, kb=kb, j=j, kslot=kslot: h.transpose(
                            pkt[kslot][:, j * 128:(j + 1) * 128],
                            kb[:, j * 128:(j + 1) * 128], idb[:]),
                            waits=(rope_done, pkt_free[kslot]) if j == 0 else ())
                    khat_free[kslot] = ttk
                    if name == "k":
                        ev = p.v(lambda h, kslot=kslot, oh=oh, sslot=sslot: h.tensor_copy(
                            stage[sslot][:, oh * 512:oh * 512 + 512], pkt[kslot][:]),
                            waits=(ttk, stage_free[sslot] if oh == 0 else None))
                        pkt_free[kslot] = ev
                        if oh == 1:
                            glob = toff + (0 if is_q_half else Q)
                            if os.environ.get("KSKIP_KT"):
                                st = p.y(lambda h, sslot=sslot: h.dma_start(
                                    out=V_s[0, :, :], in_=stage[sslot][:]),
                                    waits=(ev,))
                            else:
                                dst = KT_s[:, :, glob:glob + 128].transpose([1, 0, 2])
                                src = stage[sslot][:].rearrange("pp (dc t) -> pp dc t", dc=8)
                                st = p.y(lambda h, dst=dst, src=src: h.dma_start(
                                    out=dst, in_=src), waits=(ev,))
                            stage_free[sslot] = st
                    else:
                        ev = p.v(lambda h, kslot=kslot, oh=oh, sslot=sslot: h.tensor_copy(
                            stage[sslot][:, oh * 512:oh * 512 + 512], pkt[kslot][:]),
                            waits=(ttk, stage_free[sslot] if oh == 0 else None))
                        pkt_free[kslot] = ev
                        if oh == 1:
                            dst = QT_s[:].rearrange("pp (hp t) -> pp hp t", hp=8)
                            dst = dst[:, :, toff:toff + 128]
                            src = stage[sslot][:].rearrange("pp (dc t) -> pp dc t", dc=8)
                            st = p.y(lambda h, dst=dst, src=src: h.dma_start(
                                out=dst, in_=src), waits=(ev,))
                            stage_free[sslot] = st
        prev_wT_use = last_mm

    if _TAPS:
        dbg_xn = nc.dram_tensor("dbg_xn", [2, 128, 8 * Q], BF16,
                                kind="ExternalOutput").ap()
        dbg_wt = nc.dram_tensor("dbg_wt", [128, 8 * D], BF16,
                                kind="ExternalOutput").ap()
        p.barrier()
        p.y(lambda h: h.dma_start(out=dbg_xn[0, :, :], in_=xnq[:]))
        p.y(lambda h: h.dma_start(out=dbg_xn[1, :, :], in_=xnkv[:]))
        p.y(lambda h: h.dma_start(out=dbg_wt[:, :], in_=wTt[:]))
    p.barrier()
    p2s.close()
    xnkv_ctx.close()
    if phases <= 2:
        finish_early()
        xnq_ctx.close(); ytot_ctx.close()
        return _finalize(nc, p, sems, ctx)

    p.phase = "P3"
    p3s = ExitStack()
    e3 = p3s.enter_context
    GMAX = 5
    wg_nat = [e3(nc.sbuf_tensor("wg_nat0", [128, GMAX * D], BF16))] * 2
    wu_nat = [e3(nc.sbuf_tensor("wu_nat0", [128, GMAX * D], BF16))] * 2
    wd_nat = [e3(nc.sbuf_tensor("wd_nat0", [128, GMAX * D], BF16))] * 2
    wgTb = [e3(nc.sbuf_tensor(f"wgTb{i}", [128, GMAX * D], BF16)) for i in range(2)]
    wuTb = [e3(nc.sbuf_tensor(f"wuTb{i}", [128, GMAX * D], BF16)) for i in range(2)]
    wdTb = [e3(nc.sbuf_tensor(f"wdTb{i}", [128, GMAX * D], BF16)) for i in range(2)]
    gsil = [e3(nc.sbuf_tensor(f"gsil{i}", [128, 512], F32)) for i in range(2)]
    hsb = [e3(nc.sbuf_tensor(f"hsb{i}", [128, 512], BF16)) for i in range(GMAX)]
    pgp = e3(nc.psum_tensor("pgp", [128, 512], F32))
    pu = [e3(nc.psum_tensor(f"pu{i}", [128, 512], F32)) for i in range(2)]
    py_ = [e3(nc.psum_tensor(f"py{i}", [128, D], F32)) for i in range(2)]
    pwt = e3(nc.psum_tensor("pwt", [128, D], BF16))

    wdv = wdown.rearrange("e (dc pd) f -> e pd dc f", pd=128)
    fg_bounds = []
    f0 = 0
    for gsz in FGROUPS:
        fg_bounds.append((f0, gsz))
        f0 += gsz
    steps = [(ee, gi) for ee in range(E) for gi in range(len(FGROUPS))]

    def emit_loads(step_idx, wait0):
        ee, gi = steps[step_idx]
        fa, gsz = fg_bounds[gi]
        par = step_idx % 2
        last = None
        for fcl in range(gsz):
            fc = fa + fcl
            w = (wait0,) if fcl == 0 else ()
            p.g(lambda h, ee=ee, fc=fc, fcl=fcl, par=par: h.dma_start(
                out=wg_nat[par][:, fcl * D:(fcl + 1) * D],
                in_=wgate[ee, fc * 128:(fc + 1) * 128, :]), waits=w)
            p.g(lambda h, ee=ee, fc=fc, fcl=fcl, par=par: h.dma_start(
                out=wu_nat[par][:, fcl * D:(fcl + 1) * D],
                in_=wup[ee, fc * 128:(fc + 1) * 128, :]))
            src = wdv[ee, :, :, fc * 128:(fc + 1) * 128]
            dst = wd_nat[par][:, fcl * D:(fcl + 1) * D].rearrange(
                "pp (dc pf) -> pp dc pf", dc=8)
            last = p.g(lambda h, src=src, dst=dst: h.dma_start(out=dst, in_=src))
        return last

    def make_transpose_batches(step_idx, ld_tick, first_wait, prev_last_ev):
        ee, gi = steps[step_idx]
        fa, gsz = fg_bounds[gi]
        par = step_idx % 2
        batches = []
        state = {"prev_ev": prev_last_ev, "first": True, "last": None}

        def mk(fcl, which, nat, dstT):
            def go():
                w = (state["prev_ev"],)
                if state["first"]:
                    w = w + (ld_tick, first_wait)
                    state["first"] = False
                tk = None
                for ic in range(8):
                    tk = p.t(lambda h, ic=ic: h.transpose(
                        pwt[:, ic * 128:(ic + 1) * 128],
                        nat[par][:, fcl * D + ic * 128:fcl * D + (ic + 1) * 128],
                        idb[:]), waits=w if ic == 0 else ())
                ev = p.v(lambda h: h.tensor_copy(
                    dstT[par][:, fcl * D:(fcl + 1) * D], pwt[:]), waits=(tk,))
                state["prev_ev"] = ev
                state["last"] = ev
            return go

        for fcl in range(gsz):
            for which, (nat, dstT) in enumerate(((wg_nat, wgTb), (wu_nat, wuTb),
                                                 (wd_nat, wdTb))):
                batches.append(mk(fcl, which, nat, dstT))
        return batches, state

    ld_tick = emit_loads(0, None)
    batches, tstate = make_transpose_batches(0, ld_tick, None, None)
    for b in batches:
        b()
    T_ready = tstate["last"]
    pwt_last_ev = tstate["last"]

    hist_sil = []
    hist_hmul = []
    hist_down = []
    yev = {}
    pending_batches = []
    pending_state = None

    hs_last = [None] * GMAX
    pyfree = [None, None]
    for step_idx, (ee, gi) in enumerate(steps):
        fa, gsz = fg_bounds[gi]
        par = step_idx % 2
        if step_idx + 1 < len(steps):
            ld_next = emit_loads(step_idx + 1, T_ready)
        for qs2 in range(4):
            vm_fcl = [None] * gsz
            for fcl in range(gsz):
                w0 = (T_ready, hist_sil[-1] if hist_sil else None)
                tkg = None
                for ic in range(8):
                    tkg = p.t(lambda h, par=par, fcl=fcl, ic=ic, qs2=qs2: h.matmul(
                        pgp[:],
                        wgTb[par][:, (fcl * 8 + ic) * 128:(fcl * 8 + ic + 1) * 128],
                        xnq[:, ic * Q + qs2 * 512:ic * Q + qs2 * 512 + 512],
                        start=(ic == 0), stop=(ic == 7)),
                        waits=w0 if ic == 0 else ())
                pusel = len(hist_hmul) % 2
                w1 = (hist_hmul[-2],) if len(hist_hmul) >= 2 else ()
                tku = None
                for ic in range(8):
                    tku = p.t(lambda h, par=par, fcl=fcl, ic=ic, qs2=qs2,
                              pusel=pusel: h.matmul(
                        pu[pusel][:],
                        wuTb[par][:, (fcl * 8 + ic) * 128:(fcl * 8 + ic + 1) * 128],
                        xnq[:, ic * Q + qs2 * 512:ic * Q + qs2 * 512 + 512],
                        start=(ic == 0), stop=(ic == 7)),
                        waits=w1 if ic == 0 else ())
                gslot = len(hist_sil) % 2
                st_ = p.s(lambda h, gslot=gslot: h.activation(
                    gsil[gslot][:], pgp[:], AF.Silu),
                    waits=(tkg, hist_hmul[-2] if len(hist_hmul) >= 2 else None))
                hist_sil.append(st_)
                vm = p.v(lambda h, fcl=fcl, gslot=gslot, pusel=pusel: h.tensor_mul(
                    hsb[fcl][:], gsil[gslot][:], pu[pusel][:]),
                    waits=(st_, tku, hs_last[fcl]))
                hist_hmul.append(vm)
                vm_fcl[fcl] = vm
                if pending_batches:
                    pending_batches.pop(0)()
            for qh in range(2):
                dt = None
                for fcl in range(gsz):
                    for t2 in range(2):
                        for dh in range(2):
                            w3 = []
                            if t2 == 0 and dh == 0:
                                w3.append(vm_fcl[fcl])
                            if fcl == 0 and dh == 0:
                                w3.append(pyfree[t2])
                            dt = p.t(lambda h, fcl=fcl, t2=t2, dh=dh, par=par,
                                     qh=qh, gsz=gsz: h.matmul(
                                py_[t2][:, dh * 512:dh * 512 + 512],
                                hsb[fcl][:, qh * 256 + t2 * 128:
                                          qh * 256 + t2 * 128 + 128],
                                wdTb[par][:, fcl * D + dh * 512:
                                           fcl * D + dh * 512 + 512],
                                start=(fcl == 0), stop=(fcl == gsz - 1)),
                                waits=tuple(w3))
                    if qh == 1:
                        hs_last[fcl] = dt
                hist_down.append(dt)
                for t2 in range(2):
                    qt = qs2 * 4 + qh * 2 + t2
                    cs = comb[:, qt * E + ee:qt * E + ee + 1]
                    ysl = y_tot[:, qt * D:(qt + 1) * D]
                    if step_idx == 0:
                        vt_ = p.v(lambda h, t2=t2, cs=cs, ysl=ysl: h.tensor_scalar(
                            ysl, py_[t2][:], cs, None, ALU.mult), waits=(dt,))
                    else:
                        vt_ = p.v(lambda h, t2=t2, cs=cs, ysl=ysl:
                                  h.scalar_tensor_tensor(
                            ysl, py_[t2][:], cs, ysl, ALU.mult, ALU.add),
                            waits=(dt,))
                    pyfree[t2] = vt_
            if qs2 == 1 and step_idx + 1 < len(steps):
                pending_batches, pending_state = make_transpose_batches(
                    step_idx + 1, ld_next, dt, pwt_last_ev)
        for b in pending_batches:
            b()
        pending_batches = []
        if pending_state is not None:
            T_ready = pending_state["last"]
            pwt_last_ev = pending_state["last"]
            pending_state = None

    p.barrier()
    p3s.close()
    xnq_ctx.close()
    if _TAPS:
        dbg_y = nc.dram_tensor("dbg_y", [128, 16 * D], F32, kind="ExternalOutput").ap()
        p.y(lambda h: h.dma_start(out=dbg_y[:, :], in_=y_tot[:]))
        p.barrier()
    if phases <= 3:
        finish_early()
        ytot_ctx.close()
        return _finalize(nc, p, sems, ctx)

    osb_ctx = ExitStack()
    osb = osb_ctx.enter_context(nc.sbuf_tensor("osb", [128, 8 * Q], BF16))

    p.phase = "P4"
    if os.environ.get("KSYNC_P4"):
        p.sync_mode = True
    p4s = ExitStack()
    e4 = p4s.enter_context
    qhat = e4(nc.sbuf_tensor("qhat2", [128, 8 * Q], BF16))
    khp = [e4(nc.sbuf_tensor(f"khp{i}", [128, S], BF16)) for i in range(2)]
    vhp = [e4(nc.sbuf_tensor(f"vhp{i}", [128, 32 * 128], BF16)) for i in range(2)]
    expd = e4(nc.sbuf_tensor("expd", [128, 1024], BF16))
    red4 = e4(nc.sbuf_tensor("red4", [128, 4], F32))
    m5 = e4(nc.sbuf_tensor("m5", [128, 5], F32))
    nm4 = e4(nc.sbuf_tensor("nm4", [128, 4], F32))
    f4 = e4(nc.sbuf_tensor("f4", [128, 4], F32))
    drow = [e4(nc.sbuf_tensor(f"drow{i}", [1, 512], F32)) for i in range(2)]
    mrow = [e4(nc.sbuf_tensor(f"mrow{i}", [1, 512], F32)) for i in range(2)]
    dbc = e4(nc.sbuf_tensor("dbc", [128, 512], F32))
    mbcw = e4(nc.sbuf_tensor("mbcw", [128, 2048], F32))
    onesb = e4(nc.sbuf_tensor("onesb", [128, 1], BF16))
    subt = [e4(nc.sbuf_tensor(f"subt{i}", [128, 1024], F32)) for i in range(2)]
    pts = [e4(nc.sbuf_tensor(f"pts{i}", [128, 1024], BF16)) for i in range(2)]
    psA = [e4(nc.psum_tensor(f"psA{i}", [128, 1024], F32)) for i in range(4)]

    p.v(lambda h: h.memset(onesb[:], 1.0))
    for half in range(2):
        qre = p.y(lambda h, half=half: h.dma_start(
            out=qhat[:, half * 8192:(half + 1) * 8192],
            in_=QT_s[:, half * 8192:(half + 1) * 8192]))
    p.emit("t", None, (qre,), 0)
    psA_free = [None] * 4
    kv_free = [None, None]
    pts_free = [None] * 4
    qc_prev_s = None

    for hp in range(8):
        par = hp % 2
        kld = p.y(lambda h, hp=hp, par=par: h.dma_start(
            out=khp[par][:], in_=KT_s[hp, :, :]), waits=(kv_free[par],))
        vld = None
        for q4 in range(4):
            vsrc = V_s[q4 * 8:(q4 + 1) * 8, :, hp * 128:(hp + 1) * 128].transpose(
                [1, 0, 2])
            vld = p.y(lambda h, par=par, q4=q4, vsrc=vsrc: h.dma_start(
                out=vhp[par][:].rearrange("pp (tc t) -> pp tc t", tc=32)
                [:, q4 * 8:(q4 + 1) * 8, :], in_=vsrc))

        for head in range(2):
            hidx = hp * 2 + head
            r0 = head * 64
            for qc in range(16):
                def mm(ts):
                    mt_ = None
                    for half in range(2):
                        mt_ = p.t(lambda h, r0=r0, qc=qc, ts=ts, half=half, par=par,
                                  hp=hp: h.matmul(
                            psA[ts][:, half * 512:half * 512 + 512],
                            qhat[:, hp * Q + qc * 128:hp * Q + qc * 128 + 128]
                            [r0:r0 + 64, :],
                            khp[par][r0:r0 + 64,
                                     ts * 1024 + half * 512:ts * 1024 + half * 512 + 512],
                            start=True, stop=True),
                            waits=(kld, psA_free[ts]) if half == 0 else ())
                    return mt_

                def red(ts, mt_):
                    wv = (mt_,) + ((qc_prev_s,) if ts == 0 else ())
                    p.v(lambda h, ts=ts: h.reduce_max(
                        red4[:, ts:ts + 1], psA[ts][:], axis=AX.X), waits=wv)

                def mx(ts):
                    if ts == 0:
                        p.v(lambda h: h.tensor_copy(m5[:, 1:2], red4[:, 0:1]))
                    else:
                        p.v(lambda h, ts=ts: h.tensor_max(
                            m5[:, ts + 1:ts + 2], m5[:, ts:ts + 1],
                            red4[:, ts:ts + 1]))

                rtick = {}

                def red(ts, mt_):
                    rtick[ts] = p.v(lambda h, ts=ts: h.reduce_max(
                        red4[:, ts:ts + 1], psA[ts][:], axis=AX.X), waits=(mt_,))
                    psA_free[ts] = rtick[ts]

                red(0, mm(0))
                red(1, mm(1))
                mx(0)
                red(2, mm(2))
                mx(1)
                red(3, mm(3))
                mx(2)
                p.v(lambda h: h.tensor_copy(junk1[:], red4[:, 0:1]))
                mx(3)
                p.v(lambda h: h.tensor_copy(junk1[:], red4[:, 0:1]))
                vzr = p.v(lambda h, hidx=hidx, qc=qc: h.tensor_copy(
                    m_all[:, hidx * 16 + qc:hidx * 16 + qc + 1], m5[:, 4:5]))

        psB = [psA[0], psA[1]]
        psO = psA[2]
        psB_free = [psA_free[0], psA_free[1]]
        psO_free = psA_free[2]
        oev = None
        pv_tick = None
        subt_free = [None, None]
        row_free = vzr
        for qs in range(4):
            tprev = None
            for head in range(2):
                hidx = hp * 2 + head
                for j in range(4):
                    qc = qs * 4 + j
                    regoff = (j % 2) * 128
                    msl = m_all[:, hidx * 16 + qc:hidx * 16 + qc + 1]
                    ttd = p.t(lambda h, regoff=regoff, msl=msl: h.transpose(
                        psA[3][0:1, regoff:regoff + 128], msl, idf[:]),
                        waits=(vzr, tprev, psA_free[3], row_free))
                    tprev = p.v(lambda h, head=head, j=j, regoff=regoff: h.tensor_copy(
                        mrow[head][0:1, j * 128:(j + 1) * 128],
                        psA[3][0:1, regoff:regoff + 128]), waits=(ttd,))
            rows_done = tprev
            tb = None
            for head in range(2):
                tb = p.t(lambda h, head=head: h.matmul(
                    psA[3][:, head * 512:head * 512 + 512], ones_sb[0:1, :],
                    mrow[head][0:1, :], start=True, stop=True),
                    waits=(rows_done,) if head == 0 else ())
            vmbw = None
            for head in range(2):
                for rep in range(2):
                    vmbw = p.v(lambda h, head=head, rep=rep: h.tensor_copy(
                        mbcw[:, head * 1024 + rep * 512:
                             head * 1024 + rep * 512 + 512],
                        psA[3][:, head * 512:head * 512 + 512]),
                        waits=(tb,) if (head == 0 and rep == 0) else ())
            z_tick = None
            for tcp in range(16):
                sxs = [None, None]
                for head in range(2):
                    r0 = head * 64
                    mbt = None
                    for half in range(2):
                        tc = tcp * 2 + half
                        mbt = p.t(lambda h, head=head, r0=r0, tc=tc, qs=qs, par=par,
                                  half=half, hp=hp: h.matmul(
                            psB[head][:, half * 512:half * 512 + 512],
                            khp[par][r0:r0 + 64, tc * 128:(tc + 1) * 128],
                            qhat[:, hp * Q + qs * 512:hp * Q + qs * 512 + 512]
                            [r0:r0 + 64, :],
                            start=True, stop=True),
                            waits=(psB_free[head], vld) if half == 0 else ())
                    vsub = p.v(lambda h, head=head: h.tensor_sub(
                        subt[head][:], psB[head][:, 0:1024],
                        mbcw[:, head * 1024:head * 1024 + 1024]),
                        waits=(mbt, vmbw, subt_free[head]))
                    psB_free[head] = vsub
                    sx = p.s(lambda h, head=head: h.activation(
                        pts[head][:], subt[head][:], AF.Exp),
                        waits=(vsub, pts_free[head]))
                    subt_free[head] = sx
                    sxs[head] = sx
                for head in range(2):
                    for half in range(2):
                        tc = tcp * 2 + half
                        w = (sxs[head],) if half == 0 else ()
                        if tcp == 0 and half == 0 and head == 0:
                            w = w + (psO_free,)
                        pv_tick = p.t(lambda h, head=head, tc=tc, par=par,
                                      half=half: h.matmul(
                            psO[head * 64:head * 64 + 64,
                                head * 512:head * 512 + 512],
                            vhp[par][:, tc * 128 + head * 64:
                                     tc * 128 + head * 64 + 64],
                            pts[head][:, half * 512:half * 512 + 512],
                            start=(tc == 0), stop=(tc == 31),
                            tile_position=(0, head * 64)),
                            waits=w)
                for head in range(2):
                    for half in range(2):
                        tc = tcp * 2 + half
                        wz = (vmbw,) if (tcp == 0 and half == 0 and head == 0) else ()
                        z_tick = p.t(lambda h, head=head, half=half, tc=tc: h.matmul(
                            psA[3][0:1, head * 512:head * 512 + 512],
                            onesb[:, 0:1],
                            pts[head][:, half * 512:half * 512 + 512],
                            start=(tc == 0), stop=(tc == 31)), waits=wz)
                    pts_free[head] = z_tick
            zc0 = p.v(lambda h: h.tensor_copy(drow[0][0:1, :], psA[3][0:1, 0:512]),
                      waits=(z_tick,))
            p.v(lambda h: h.tensor_copy(drow[1][0:1, :], psA[3][0:1, 512:1024]))
            p.v(lambda h: h.reciprocal(drow[0][0:1, :], drow[0][0:1, :]))
            rc1 = p.v(lambda h: h.reciprocal(drow[1][0:1, :], drow[1][0:1, :]))
            tb2 = None
            for head in range(2):
                tb2 = p.t(lambda h, head=head: h.matmul(
                    psA[3][head * 64:head * 64 + 64, 0:512], ones_sb[0:1, 0:64],
                    drow[head][0:1, :], start=True, stop=True,
                    tile_position=(0, head * 64)),
                    waits=(rc1,) if head == 0 else ())
            vdb = p.v(lambda h: h.tensor_copy(dbc[:], psA[3][:, 0:512]), waits=(tb2,))
            psA_free[3] = vdb
            row_free = tb2
            p.v(lambda h, hp=hp, qs=qs: h.tensor_mul(
                osb[0:64, hp * Q + qs * 512:hp * Q + qs * 512 + 512],
                psO[0:64, 0:512], dbc[0:64, :]), waits=(pv_tick, vdb))
            oev = p.v(lambda h, hp=hp, qs=qs: h.tensor_mul(
                osb[64:128, hp * Q + qs * 512:hp * Q + qs * 512 + 512],
                psO[64:128, 512:1024], dbc[64:128, :]))
            psO_free = oev
        psA_free[0] = psB_free[0]
        psA_free[1] = psB_free[1]
        psA_free[2] = oev
        kv_free[par] = pv_tick

    if not os.environ.get("KSYNC_ALL"):
        p.sync_mode = False
    p.barrier()
    p4s.close()
    if _TAPS:
        dbg_m = nc.dram_tensor("dbg_m", [128, NH * 16], F32, kind="ExternalOutput").ap()
        dbg_z = nc.dram_tensor("dbg_z", [128, NH * 16], F32, kind="ExternalOutput").ap()
        dbg_o = nc.dram_tensor("dbg_o", [128, 8 * Q], BF16, kind="ExternalOutput").ap()
        p.y(lambda h: h.dma_start(out=dbg_m[:, :], in_=m_all[:]))
        p.y(lambda h: h.dma_start(out=dbg_z[:, :], in_=z_all[:]))
        p.y(lambda h: h.dma_start(out=dbg_o[:, :], in_=osb[:]))
        p.barrier()
    if phases <= 4:
        finish_early()
        osb_ctx.close(); ytot_ctx.close()
        return _finalize(nc, p, sems, ctx)

    p.phase = "P5"
    p5s = ExitStack()
    e5 = p5s.enter_context
    woT = e5(nc.sbuf_tensor("woT", [128, 8 * D], BF16))
    wnat2 = [e5(nc.sbuf_tensor(f"wnat2{i}", [128, D], BF16)) for i in range(2)]
    xtile = [e5(nc.sbuf_tensor(f"xtile{i}", [128, D], F32)) for i in range(2)]
    gtile = [e5(nc.sbuf_tensor(f"gtile{i}", [128, D], F32)) for i in range(2)]
    otile = [e5(nc.sbuf_tensor(f"otile{i}", [128, D], F32)) for i in range(2)]
    pso = [e5(nc.psum_tensor(f"pso{i}", [128, 512], F32)) for i in range(2)]
    pw5 = e5(nc.psum_tensor("pw5", [128, D], BF16))

    wt5 = []
    for oc in range(8):
        slot = oc % 2
        gld = p.g(lambda h, oc=oc, slot=slot: h.dma_start(
            out=wnat2[slot][:], in_=Wo[oc * 128:(oc + 1) * 128, :]),
            waits=(wt5[oc - 2] if oc >= 2 else None,))
        tk = None
        for ic in range(8):
            tk = p.t(lambda h, slot=slot, ic=ic: h.transpose(
                pw5[:, ic * 128:(ic + 1) * 128],
                wnat2[slot][:, ic * 128:(ic + 1) * 128], idb[:]),
                waits=(gld, wt5[oc - 1] if oc >= 1 else None) if ic == 0 else ())
        dv = woT[:].rearrange("pp (ic o) -> pp ic o", ic=8)
        pv = pw5[:].rearrange("pp (ic o) -> pp ic o", ic=8)
        wt5.append(p.v(lambda h, dv=dv, pv=pv, oc=oc: h.tensor_copy(
            dv[:, :, oc * 128:(oc + 1) * 128], pv), waits=(tk,)))

    slot_rel = {}
    pso_free = [None, None]
    out_dma = {}
    sg_hist = {}
    for tokc in range(16):
        slot = tokc % 2
        xld = p.y(lambda h, tokc=tokc, slot=slot: h.dma_start(
            out=xtile[slot][:], in_=xi[tokc * 128:(tokc + 1) * 128, :]),
            waits=(slot_rel.get(tokc - 2),))
        sg = p.s(lambda h, slot=slot, tokc=tokc: h.activation(
            gtile[slot][:], y_tot[:, tokc * D:(tokc + 1) * D], AF.Silu),
            waits=(slot_rel.get(tokc - 2),))
        sg_hist[tokc] = sg
        vv2 = None
        for oh in range(2):
            tk = None
            for hpc in range(8):
                tk = p.t(lambda h, tokc=tokc, hpc=hpc, oh=oh: h.matmul(
                    pso[oh][:],
                    osb[:, hpc * Q + tokc * 128:hpc * Q + tokc * 128 + 128],
                    woT[:, hpc * D + oh * 512:hpc * D + oh * 512 + 512],
                    start=(hpc == 0), stop=(hpc == 7)),
                    waits=(wt5[-1], pso_free[oh]) if hpc == 0 else ())
            vv1 = p.v(lambda h, slot=slot, oh=oh: h.tensor_mul(
                otile[slot][:, oh * 512:oh * 512 + 512], pso[oh][:],
                gtile[slot][:, oh * 512:oh * 512 + 512]),
                waits=(tk, sg, xld, out_dma.get(tokc - 2)))
            pso_free[oh] = vv1
            vv2 = p.v(lambda h, slot=slot, oh=oh: h.tensor_add(
                otile[slot][:, oh * 512:oh * 512 + 512],
                otile[slot][:, oh * 512:oh * 512 + 512],
                xtile[slot][:, oh * 512:oh * 512 + 512]))
        slot_rel[tokc] = vv2
        out_dma[tokc] = p.y(lambda h, tokc=tokc, slot=slot: h.dma_start(
            out=out[tokc * 128:(tokc + 1) * 128, :], in_=otile[slot][:]),
            waits=(vv2,))

    p.emit("y", None, (out_dma[15],), 0)
    p5s.close()
    osb_ctx.close()
    ytot_ctx.close()

    return _finalize(nc, p, sems, ctx)


_CACHED = {}


def kernel(L0_attn_hidden_states, L0_attn_Wq, L0_attn_Wk, L0_attn_Wv, L0_attn_Wo,
           L0_ffn_gate_w, L0_ffn_w_gate, L0_ffn_w_up, L0_ffn_w_down):
    x = np.ascontiguousarray(L0_attn_hidden_states)
    B = x.shape[0]
    cos, sin = rope_tables()
    identb = np.eye(128, dtype=ml_dtypes.bfloat16)
    identf = np.eye(128, dtype=np.float32)
    onesf = np.ones((1, 128), dtype=np.float32)

    if "nc" not in _CACHED:
        _CACHED["nc"] = build_program()
    nc = _CACHED["nc"]

    shared = {
        "Wq": np.ascontiguousarray(L0_attn_Wq),
        "Wk": np.ascontiguousarray(L0_attn_Wk),
        "Wv": np.ascontiguousarray(L0_attn_Wv),
        "Wo": np.ascontiguousarray(L0_attn_Wo),
        "gwTin": np.ascontiguousarray(
            L0_ffn_gate_w.reshape(E, 8, 128).transpose(2, 1, 0).reshape(128, 8 * E)
            .astype(np.float32)),
        "w_gate": np.ascontiguousarray(L0_ffn_w_gate),
        "w_up": np.ascontiguousarray(L0_ffn_w_up),
        "w_down": np.ascontiguousarray(L0_ffn_w_down),
        "identb": identb, "identf": identf, "onesf": onesf,
    }
    in_maps = []
    for c in range(8):
        b, hh = c // 2, c % 2
        xq = x[b, hh * Q:(hh + 1) * Q, :]
        xo = x[b, (1 - hh) * Q:(2 - hh) * Q, :]
        ck = np.concatenate([cos[hh * Q:(hh + 1) * Q], cos[(1 - hh) * Q:(2 - hh) * Q]])
        sk = np.concatenate([sin[hh * Q:(hh + 1) * Q], sin[(1 - hh) * Q:(2 - hh) * Q]])
        m = dict(shared)
        m["x_full"] = np.ascontiguousarray(np.concatenate([xq, xo], axis=0))
        m["cosq8"] = np.ascontiguousarray(8.0 * cos[hh * Q:(hh + 1) * Q])
        m["sinq8"] = np.ascontiguousarray(8.0 * sin[hh * Q:(hh + 1) * Q])
        m["cosk"] = np.ascontiguousarray(ck)
        m["sink"] = np.ascontiguousarray(sk)
        in_maps.append(m)

    res = run_bass_kernel_spmd(nc, in_maps, core_ids=list(range(8)))
    outp = np.empty((B, S, D), dtype=np.float32)
    for c in range(8):
        b, hh = c // 2, c % 2
        outp[b, hh * Q:(hh + 1) * Q, :] = res.results[c]["out"]
    return outp
